# version 07
# baseline (speedup 1.0000x reference)
import numpy as np
import ml_dtypes
import concourse.bass as bass
import concourse.mybir as mybir
from concourse.bass_utils import run_bass_kernel_spmd
from concourse.alu_op_type import AluOpType

F32 = mybir.dt.float32
BF16 = mybir.dt.bfloat16
AF = mybir.ActivationFunctionType
AX = mybir.AxisListType
BF16NP = ml_dtypes.bfloat16

S = 2048
D = 1024
NT = 16
H = 16
HD = 64
NPAIR = 8
DFF0 = 2816
NE = 8
DFF1 = 4096

ENGS = ("sync", "gpsimd", "tensor", "vector", "scalar")
DMA_ENGS = ("sync", "gpsimd")


class Prog:

    def __init__(self):
        self.eops = {e: [] for e in ENGS}
        self.lastw = {}
        self.readers = {}

    def add(self, eng, emit, reads=(), writes=(), tag=""):
        idx = len(self.eops[eng])
        deps = {}

        def need(e2, i2):
            if e2 == eng:
                if eng == "tensor":
                    return
            if deps.get(e2, -1) < i2:
                deps[e2] = i2

        for r in reads:
            lw = self.lastw.get(r)
            if lw is not None:
                need(*lw)
        for w in writes:
            lw = self.lastw.get(w)
            if lw is not None:
                need(*lw)
            for e2, i2 in self.readers.get(w, {}).items():
                need(e2, i2)
        self.eops[eng].append([emit, deps, eng in DMA_ENGS, tag])
        for r in reads:
            self.readers.setdefault(r, {})
            if self.readers[r].get(eng, -1) < idx:
                self.readers[r][eng] = idx
        for w in writes:
            self.lastw[w] = (eng, idx)
            self.readers[w] = {eng: idx}

    def finalize(self, nc, block, sems):
        marked = {e: set() for e in ENGS}
        for e in ENGS:
            for op in self.eops[e]:
                for e2, i2 in op[1].items():
                    marked[e2].add(i2)
        semval = {}
        for e in ENGS:
            if e in DMA_ENGS:
                semval[e] = [16 * (i + 1) for i in range(len(self.eops[e]))]
            else:
                c = 0
                vals = []
                for i in range(len(self.eops[e])):
                    if i in marked[e]:
                        c += 1
                    vals.append(c)
                semval[e] = vals
        totals = {e: (semval[e][-1] if self.eops[e] else 0) for e in ENGS}

        def body(ename):
            ops = self.eops[ename]
            mk = marked[ename]

            def f(eng):
                water = {e: 0 for e in ENGS}
                for i, (emit, deps, is_dma, tag) in enumerate(ops):
                    for e2, i2 in sorted(deps.items()):
                        v = semval[e2][i2]
                        if water[e2] < v:
                            eng.wait_ge(sems[e2], v)
                            water[e2] = v
                    inst = emit(eng)
                    if is_dma:
                        inst.then_inc(sems[ename], 16)
                    elif i in mk:
                        inst.then_inc(sems[ename], 1)
                if ename == "sync":
                    for e2 in ENGS:
                        if totals[e2] > 0 and e2 != "sync":
                            eng.wait_ge(sems[e2], totals[e2])
                    if totals["sync"] > 0:
                        eng.wait_ge(sems["sync"], totals["sync"])
            return f

        for e in ENGS:
            getattr(block, e)(body(e))


BLK = 2048


SBUF_BASE = 16544
SBUF_LIMIT = 229376


class Mem:
    def __init__(self, nc):
        self.nc = nc
        self.off = SBUF_BASE
        self.tensors = {}

    def at(self, name, shape, dtype, offset=None):
        nbytes = shape[-1] * (2 if dtype == BF16 else 4)
        if len(shape) > 2:
            for s in shape[1:-1]:
                nbytes *= s
        if offset is None:
            offset = self.off
            self.off = offset + ((nbytes + 31) // 32 * 32)
        t = self.nc.alloc_sbuf_tensor_at(name, shape, dtype, offset=offset)
        self.tensors[name] = (t, offset, nbytes)
        return t

    def res(self, name, col0=None, col1=None):
        t, off, nbytes = self.tensors[name]
        esz = 2 if t.dtype == BF16 else 4
        b0 = off if col0 is None else off + col0 * esz
        b1 = off + nbytes if col1 is None else off + col1 * esz
        return tuple(f"sb{b}" for b in range(b0 // BLK, (b1 - 1) // BLK + 1))


def sres(mem, name, c0=None, c1=None):
    return mem.res(name, c0, c1)


def build_nc(layers=(0, 1), taps=None):
    nc = bass.Bass("TRN2", target_bir_lowering=False)
    P = Prog()

    din = {}

    def dram_in(name, shape, dt=F32):
        din[name] = nc.dram_tensor(name, shape, dt, kind="ExternalInput").ap()
        return din[name]

    X0 = dram_in("x0", [S, D])
    out = nc.dram_tensor("out", [S, D], F32, kind="ExternalOutput").ap()
    dx1 = nc.dram_tensor("dx1", [S, D], F32, kind="Internal").ap()

    if 0 in layers:
        for nm, sh in [("Wq", [D, D]), ("Wk", [D, D]), ("Wv", [D, D]), ("Wo", [D, D]),
                       ("bq", [D]), ("bk", [D]), ("bv", [D]), ("bo", [D]),
                       ("w1", [NE, D, DFF0]), ("w3", [NE, D, DFF0]), ("w2", [NE, DFF0, D])]:
            dram_in("L0_" + nm, sh)
    if 1 in layers:
        for nm, sh in [("wqkv", [D, 3 * D]), ("bqkv", [3 * D]), ("wout", [D, D]), ("bout", [D]),
                       ("cent", [NE, D]), ("W1", [NE, D, DFF1]), ("b1", [NE, DFF1]),
                       ("W2", [NE, DFF1, D]), ("b2", [NE, D])]:
            dram_in("L1_" + nm, sh)
    dram_in("c_ident_bf", [128, 128], BF16)
    dram_in("c_ident_f32", [128, 128], F32)
    dram_in("c_mask0", [128, 128], F32)
    dram_in("c_mask1", [128, 128], F32)
    dram_in("c_ones_row", [1, 512], BF16)
    dram_in("c_seedq", [1, 2], BF16)
    dram_in("c_seedk", [1, 2], BF16)
    dram_in("c_selq", [128, 4], BF16)
    dram_in("c_selk", [128, 4], BF16)
    dram_in("c_idx", [128, 8], F32)

    taps = taps or []

    from contextlib import ExitStack
    stack = ExitStack()
    mem = Mem(nc)

    identb = mem.at("identb", [128, 128], BF16)
    identf = mem.at("identf", [128, 128], F32)
    mask0 = mem.at("mask0", [128, 128], F32)
    mask1 = mem.at("mask1", [128, 128], F32)
    onesrow = mem.at("onesrow", [1, 512], BF16)
    seedq = mem.at("seedq", [1, 2], BF16)
    seedk = mem.at("seedk", [1, 2], BF16)
    selq = mem.at("selq", [128, 4], BF16)
    selk = mem.at("selk", [128, 4], BF16)
    ceps = mem.at("ceps", [128, 1], F32)
    cidx = mem.at("cidx", [128, 8], F32)
    gscr = mem.at("gscr", [128, 8], F32)
    bqk = mem.at("bqk", [128, 32], F32)
    b1sb = mem.at("b1sb", [128, 256], F32)
    btmp = mem.at("btmp", [32, 128], F32)
    bvc = mem.at("bvc", [128, 8], BF16)
    borow = mem.at("borow", [1, 1024], BF16)
    bprow = mem.at("bprow", [1, 1024], BF16)
    centb = mem.at("centb", [8, 1024], F32)
    centT = mem.at("centT", [128, 64], F32)
    mv = mem.at("mv", [128, 4], F32)
    bnst = mem.at("bnst", [128, 12], F32)
    smax = mem.at("smax", [128, 64], F32)
    gat = mem.at("gat", [128, 128], F32)
    gfin = mem.at("gfin", [128, 128], F32)
    xnb = mem.at("xnb", [128, 1024], BF16)
    mem.off = (mem.off + BLK - 1) // BLK * BLK
    XnT = mem.at("XnT", [128, 8 * S], BF16)
    g = mem.at("g", [128, NT * D], BF16)
    xbuf = mem.at("xbuf", [128, 2 * D], F32)
    arena0 = mem.off
    mem.off = arena0
    w1b = mem.at("w1b", [128, 2 * 4096], BF16)
    w3b = mem.at("w3b", [128, 2 * 4096], BF16)
    w2b = mem.at("w2b", [128, 2 * 4096], BF16)
    hT0 = mem.at("hT0", [128, 22 * 256], BF16)
    hs0 = mem.at("hs0", [128, 2 * 256], BF16)
    mem.off = arena0
    W2sb = mem.at("W2sb", [128, 32 * 1024], BF16)
    xcb = mem.at("xcb", [128, 1024], F32, offset=mem.tensors["W2sb"][1])
    xcT = mem.at("xcT", [128, 1024], F32, offset=mem.tensors["W2sb"][1] + 4096)
    hTm = mem.at("hTm", [128, 32 * 512], BF16)
    w1mb = mem.at("w1mb", [128, 2 * 4096], BF16)
    b2row = mem.at("b2row", [1, 2048], BF16)
    mem.off = arena0
    QTz0 = mem.at("QTz0", [128, S], BF16)
    QTz1 = mem.at("QTz1", [128, S], BF16)
    KTA0 = mem.at("KTA0", [128, S], BF16)
    KTA1 = mem.at("KTA1", [128, S], BF16)
    MA0 = mem.at("MA0", [128, S], BF16)
    MA1 = mem.at("MA1", [128, S], BF16)
    Vsb = mem.at("Vsb", [128, S], BF16)
    Wosb = mem.at("Wosb", [128, 8 * 1024], BF16, offset=mem.tensors["QTz0"][1])
    wqb = mem.at("wqb", [128, 1024], BF16)
    wkb = mem.at("wkb", [128, 1024], BF16)
    wvb = mem.at("wvb", [128, 1024], BF16)
    scor = {}
    Ebuf = {}
    ETb = {}
    sqd = {}
    rdb = {}
    for h in (0, 1):
        scor[h] = mem.at(f"scor{h}", [128, S], F32)
        for par in (0, 1):
            Ebuf[(h, par)] = mem.at(f"E{h}{par}", [128, S], BF16)
        ETb[h] = mem.at(f"ET{h}", [128, S], BF16)
    OTall = mem.at("OTall", [128, 8 * S], BF16)
    gs = mem.at("gs", [128, 1024], F32, offset=mem.tensors["E00"][1])
    t1 = mem.at("t1", [128, 1024], F32, offset=mem.tensors["E01"][1])
    bobc = mem.at("bobc", [128, D], F32, offset=mem.tensors["E10"][1])
    rdb[0] = mem.at("rd0", [128, 512], F32, offset=mem.tensors["centb"][1])
    rdb[1] = mem.at("rd1", [128, 512], F32, offset=mem.tensors["centb"][1] + 2048)
    sqsb = mem.at("sqsb", [128, S], BF16, offset=mem.tensors["scor0"][1])
    assert mem.off <= SBUF_LIMIT, f"SBUF overflow: {mem.off}"

    pk = [stack.enter_context(nc.psum_tensor(f"pk{i}", [128, 512], F32)) for i in range(8)]
    pb = [t.bitcast(BF16) for t in pk]

    sems = {}
    for e in ENGS:
        sems[e] = stack.enter_context(nc.semaphore("sem_" + e))
    block = stack.enter_context(nc.Block())

    PS = lambda i: (f"ps{i}",)

    def dma(queue, outap, inap, reads, writes, tag=""):
        P.add(queue, lambda e, o=outap, i=inap: e.dma_start(out=o, in_=i),
              reads=reads, writes=writes, tag=tag)

    def MM(lhsT, rhs, outp, start, stop, reads, writes, tag=""):
        P.add("tensor",
              lambda e, a=lhsT, b=rhs, o=outp, s1=start, s2=stop:
                  e.matmul(o, a, b, start=s1, stop=s2),
              reads=reads, writes=writes, tag=tag)

    dma("sync", identb[:], din["c_ident_bf"], (), sres(mem, "identb"))
    dma("sync", identf[:], din["c_ident_f32"], (), sres(mem, "identf"))
    dma("sync", mask0[:], din["c_mask0"], (), sres(mem, "mask0"))
    dma("sync", mask1[:], din["c_mask1"], (), sres(mem, "mask1"))
    dma("sync", onesrow[:], din["c_ones_row"], (), sres(mem, "onesrow"))
    dma("sync", seedq[:], din["c_seedq"], (), sres(mem, "seedq"))
    dma("sync", seedk[:], din["c_seedk"], (), sres(mem, "seedk"))
    dma("sync", selq[:], din["c_selq"], (), sres(mem, "selq"))
    dma("sync", selk[:], din["c_selk"], (), sres(mem, "selk"))
    P.add("vector", lambda e: e.memset(ceps[:], 1e-5), (), sres(mem, "ceps"))
    dma("sync", cidx[:], din["c_idx"], (), sres(mem, "cidx"))

    def load_bias_cols(dsrc, dstap, dst_res, n):
        dma("sync", btmp[0:n, :], dsrc.rearrange("(a b) -> a b", b=128),
            sres(mem, "btmp"), sres(mem, "btmp"))
        P.add("tensor", lambda e, nn=n: e.transpose(pk[7][0:128, 0:nn], btmp[0:nn, :], identf[0:nn, 0:nn]),
              reads=sres(mem, "btmp") + sres(mem, "identf"), writes=PS(7))
        P.add("vector", lambda e, d=dstap, nn=n: e.tensor_copy(d, pk[7][:, 0:nn]),
              reads=PS(7), writes=dst_res)

    if 0 in layers:
        load_bias_cols(din["L0_bq"], bqk[:, 0:8], sres(mem, "bqk"), 8)
        load_bias_cols(din["L0_bk"], bqk[:, 8:16], sres(mem, "bqk"), 8)
    if 1 in layers:
        load_bias_cols(din["L1_bqkv"][0:1024], bqk[:, 16:24], sres(mem, "bqk"), 8)
        load_bias_cols(din["L1_bqkv"][1024:2048], bqk[:, 24:32], sres(mem, "bqk"), 8)
        for e_ in range(NE):
            load_bias_cols(din["L1_b1"][e_], b1sb[:, 32 * e_:32 * e_ + 32],
                           sres(mem, "b1sb"), 32)
        dma("sync", centb[:], din["L1_cent"], (), sres(mem, "centb"))
        for k in range(8):
            P.add("tensor", lambda e, k=k: e.transpose(pk[7][0:128, 0:8],
                                                       centb[:, 128 * k:128 * k + 128],
                                                       identf[0:8, 0:8]),
                  reads=sres(mem, "centb") + sres(mem, "identf"), writes=PS(7))
            P.add("vector", lambda e, k=k: e.tensor_copy(centT[:, 8 * k:8 * k + 8], pk[7][:, 0:8]),
                  reads=PS(7), writes=sres(mem, "centT"))

    def emit_ln(L):
        xsrc = X0 if L == 0 else dx1
        xres = (lambda t: (f"dx0.{t}",)) if L == 0 else (lambda t: (f"dx1.{t}",))
        for t in range(NT):
            xb = xbuf.ap()[:, (t % 2) * D:(t % 2) * D + D]
            xrs = sres(mem, "xbuf", (t % 2) * D, (t % 2) * D + D)
            dma("sync", xb, xsrc[128 * t:128 * t + 128, :], xres(t), xrs, tag=f"lnx{t}")
            P.add("vector", lambda e, xb=xb: e.bn_stats(bnst[:, 0:6], xb[:, 0:512]),
                  reads=xrs, writes=sres(mem, "bnst"))
            P.add("vector", lambda e, xb=xb: e.bn_stats(bnst[:, 6:12], xb[:, 512:1024]),
                  reads=xrs, writes=sres(mem, "bnst"))
            P.add("vector", lambda e: e.bn_aggr(mv[:, 0:2], bnst[:]),
                  reads=sres(mem, "bnst"), writes=sres(mem, "mv"))
            P.add("scalar", lambda e: e.activation(mv[:, 2:3], mv[:, 1:2], AF.Sqrt, bias=ceps[:]),
                  reads=sres(mem, "mv") + sres(mem, "ceps"), writes=sres(mem, "mv"))
            P.add("vector", lambda e: e.reciprocal(mv[:, 3:4], mv[:, 2:3]),
                  reads=sres(mem, "mv"), writes=sres(mem, "mv"))
            if L == 0:
                P.add("vector", lambda e, xb=xb: e.tensor_scalar(
                    xnb[:], xb, mv[:, 0:1], mv[:, 3:4], op0=AluOpType.subtract, op1=AluOpType.mult),
                    reads=xrs + sres(mem, "mv"), writes=sres(mem, "xnb"))
            else:
                P.add("vector", lambda e, xb=xb: e.tensor_scalar(
                    xcb[:], xb, mv[:, 0:1], None, op0=AluOpType.subtract),
                    reads=xrs + sres(mem, "mv"), writes=sres(mem, "xcb"))
                P.add("vector", lambda e: e.tensor_scalar(
                    xnb[:], xcb[:], mv[:, 3:4], None, op0=AluOpType.mult),
                    reads=sres(mem, "xcb") + sres(mem, "mv"), writes=sres(mem, "xnb"))
            bank = 6 + (t % 2)
            for c in range(8):
                MM0 = lambda e, c=c, bank=bank: e.transpose(
                    pb[bank][:, 128 * c:128 * c + 128], xnb[:, 128 * c:128 * c + 128], identb[:])
                P.add("tensor", MM0, reads=sres(mem, "xnb") + sres(mem, "identb"), writes=PS(bank))
            for c in range(8):
                P.add("vector", lambda e, c=c, t=t, bank=bank: e.tensor_copy(
                    XnT[:, 2048 * c + 128 * t:2048 * c + 128 * t + 128],
                    pb[bank][:, 128 * c:128 * c + 128]),
                    reads=PS(bank), writes=sres(mem, "XnT", 2048 * c + 128 * t, 2048 * c + 128 * t + 128))
            if L == 1:
                fb = 4 + (t % 2)
                for c in range(8):
                    P.add("tensor", lambda e, c=c, fb=fb: e.transpose(
                        pk[fb][:, 128 * (c % 4):128 * (c % 4) + 128].rearrange("p (a b) -> p a b", a=1)[:, 0, :]
                        if False else pk[fb][:, 128 * (c % 4):128 * (c % 4) + 128],
                        xcb[:, 128 * c:128 * c + 128], identf[:]),
                        reads=sres(mem, "xcb") + sres(mem, "identf"), writes=PS(fb))
                    if c % 4 == 3:
                        P.add("vector", lambda e, c=c, fb=fb: e.tensor_copy(
                            xcT[:, 512 * (c // 4):512 * (c // 4) + 512], pk[fb][:, 0:512]),
                            reads=PS(fb), writes=sres(mem, "xcT"))
                for c in range(8):
                    MM(xcT[:, 128 * c:128 * c + 128], centT[:, 8 * c:8 * c + 8],
                       pk[2][:, 0:8], c == 0, c == 7,
                       sres(mem, "xcT") + sres(mem, "centT"), PS(2))
                gA = gat.ap()[:, 8 * t:8 * t + 8]
                gF = gfin.ap()[:, 8 * t:8 * t + 8]
                gAr = sres(mem, "gat", 8 * t, 8 * t + 8)
                gFr = sres(mem, "gfin", 8 * t, 8 * t + 8)
                sm = smax.ap()
                smr = sres(mem, "smax")
                P.add("vector", lambda e: e.tensor_scalar(
                    mv[:, 2:3], mv[:, 3:4], -1.0, None, op0=AluOpType.mult),
                    reads=sres(mem, "mv"), writes=sres(mem, "mv"))
                P.add("scalar", lambda e, gA=gA: e.activation(gA, pk[2][:, 0:8], AF.Exp,
                                                              scale=mv[:, 2:3]),
                      reads=PS(2) + sres(mem, "mv"), writes=gAr)
                P.add("vector", lambda e, gA=gA: e.tensor_scalar(
                    gA, gA, 1.0, None, op0=AluOpType.add), reads=gAr, writes=gAr)
                P.add("vector", lambda e, gA=gA: e.reciprocal(gA, gA), reads=gAr, writes=gAr)
                gsr = sres(mem, "gscr")
                cir = sres(mem, "cidx")
                V = "vector"
                P.add(V, lambda e, gA=gA, sm=sm: e.reduce_max(sm[:, 0:1], gA, axis=AX.X),
                      reads=gAr, writes=smr)
                P.add(V, lambda e, gA=gA, gF=gF, sm=sm: e.tensor_scalar(
                    gF, gA, sm[:, 0:1], None, op0=AluOpType.is_equal), reads=gAr + smr, writes=gFr)
                P.add(V, lambda e, gF=gF: e.scalar_tensor_tensor(
                    gF, gF, -1000.0, cidx[:], op0=AluOpType.mult, op1=AluOpType.add),
                    reads=gFr + cir, writes=gFr)
                P.add(V, lambda e, gF=gF, sm=sm: e.tensor_reduce(
                    sm[:, 1:2], gF, axis=AX.X, op=AluOpType.min), reads=gFr, writes=smr)
                P.add(V, lambda e, gF=gF, sm=sm: e.tensor_scalar(
                    gF, gF, sm[:, 1:2], None, op0=AluOpType.is_equal), reads=gFr + smr, writes=gFr)
                P.add(V, lambda e, gA=gA, gF=gF: e.scalar_tensor_tensor(
                    gscr[:], gF, -1e30, gA, op0=AluOpType.mult, op1=AluOpType.add),
                    reads=gAr + gFr, writes=gsr)
                P.add(V, lambda e, sm=sm: e.reduce_max(sm[:, 2:3], gscr[:], axis=AX.X),
                      reads=gsr, writes=smr)
                P.add(V, lambda e, sm=sm: e.tensor_scalar(
                    gscr[:], gscr[:], sm[:, 2:3], None, op0=AluOpType.is_equal),
                    reads=gsr + smr, writes=gsr)
                P.add(V, lambda e: e.scalar_tensor_tensor(
                    gscr[:], gscr[:], -1000.0, cidx[:], op0=AluOpType.mult, op1=AluOpType.add),
                    reads=gsr + cir, writes=gsr)
                P.add(V, lambda e, sm=sm: e.tensor_reduce(
                    sm[:, 3:4], gscr[:], axis=AX.X, op=AluOpType.min), reads=gsr, writes=smr)
                P.add(V, lambda e, sm=sm: e.tensor_scalar(
                    gscr[:], gscr[:], sm[:, 3:4], None, op0=AluOpType.is_equal),
                    reads=gsr + smr, writes=gsr)
                P.add(V, lambda e, gF=gF: e.tensor_tensor(gF, gF, gscr[:], op=AluOpType.add),
                      reads=gFr + gsr, writes=gFr)
                P.add(V, lambda e, sm=sm: e.tensor_tensor(
                    sm[:, 4:5], sm[:, 0:1], sm[:, 2:3], op=AluOpType.add), reads=smr, writes=smr)
                P.add(V, lambda e, sm=sm: e.reciprocal(sm[:, 5:6], sm[:, 4:5]),
                      reads=smr, writes=smr)
                P.add(V, lambda e, gA=gA, gF=gF, sm=sm: e.scalar_tensor_tensor(
                    gF, gA, sm[:, 5:6], gF, op0=AluOpType.mult, op1=AluOpType.mult),
                    reads=gAr + smr + gFr, writes=gFr)

    def emit_ffn0():
        NKG1 = 6
        for e_ in range(NE):
            tok0 = 256 * e_
            for gi in range(NKG1):
                cw = 512 if gi < 5 else 256
                buf = gi % 2
                for nm, dsrc in (("w1b", din["L0_w1"]), ("w3b", din["L0_w3"])):
                    dma("gpsimd",
                        mem.tensors[nm][0].ap().rearrange("p (b k m) -> p b k m", b=2, k=8)[:, buf, :, 0:cw],
                        dsrc[e_][:, 512 * gi:512 * gi + cw].rearrange("(k p) m -> p k m", p=128),
                        sres(mem, nm, buf * 4096, buf * 4096 + 4096),
                        sres(mem, nm, buf * 4096, buf * 4096 + 4096), tag=f"ffn0w{e_}")
                for mc in range(4 if gi < 5 else 2):
                    mct = 4 * gi + mc
                    bnk = mct % 2
                    bnk2 = 2 + mct % 2
                    for k in range(8):
                        MM(w1b.ap()[:, 4096 * buf + 512 * k + 128 * mc: 4096 * buf + 512 * k + 128 * mc + 128],
                           XnT[:, 2048 * k + tok0:2048 * k + tok0 + 256],
                           pk[bnk][:, 0:256], k == 0, k == 7,
                           sres(mem, "w1b", buf * 4096, buf * 4096 + 4096) +
                           sres(mem, "XnT", 2048 * k + tok0, 2048 * k + tok0 + 256),
                           PS(bnk))
                    for k in range(8):
                        MM(w3b.ap()[:, 4096 * buf + 512 * k + 128 * mc: 4096 * buf + 512 * k + 128 * mc + 128],
                           XnT[:, 2048 * k + tok0:2048 * k + tok0 + 256],
                           pk[bnk2][:, 0:256], k == 0, k == 7,
                           sres(mem, "w3b", buf * 4096, buf * 4096 + 4096) +
                           sres(mem, "XnT", 2048 * k + tok0, 2048 * k + tok0 + 256),
                           PS(bnk2))
                    hsap = hs0.ap()[:, 256 * (mct % 2):256 * (mct % 2) + 256]
                    hsres = sres(mem, "hs0", 256 * (mct % 2), 256 * (mct % 2) + 256)
                    P.add("scalar", lambda e, b=bnk, hsap=hsap: e.activation(hsap, pk[b][:, 0:256], AF.Silu),
                          reads=PS(bnk), writes=hsres)
                    P.add("vector", lambda e, b=bnk2, mct=mct, hsap=hsap: e.tensor_tensor(
                        hT0[:, 256 * mct:256 * mct + 256], hsap, pk[b][:, 0:256], op=AluOpType.mult),
                        reads=hsres + PS(bnk2), writes=sres(mem, "hT0", 256 * mct, 256 * mct + 256))
            for gi in range(NKG1):
                cw = 512 if gi < 5 else 256
                buf = gi % 2
                dma("gpsimd",
                    w2b.ap().rearrange("p (b k m) -> p b k m", b=2, k=4)[:, buf, 0:cw // 128, :],
                    din["L0_w2"][e_][512 * gi:512 * gi + cw, :].rearrange("(k p) m -> p k m", p=128),
                    sres(mem, "w2b", buf * 4096, buf * 4096 + 4096),
                    sres(mem, "w2b", buf * 4096, buf * 4096 + 4096), tag=f"ffn0w2{e_}")
                for j in range(2):
                    tt = 2 * e_ + j
                    pbk = [4 + j, 6 + j]
                    for kc in range(cw // 128):
                        mct = 4 * gi + kc
                        for nh in range(2):
                            MM(hT0[:, 256 * mct + 128 * j:256 * mct + 128 * j + 128],
                               w2b.ap()[:, 4096 * buf + 1024 * kc + 512 * nh:4096 * buf + 1024 * kc + 512 * nh + 512],
                               pk[pbk[nh]][:, 0:512], gi == 0 and kc == 0, gi == NKG1 - 1 and kc == (cw // 128) - 1,
                               sres(mem, "hT0", 256 * mct, 256 * mct + 256) +
                               sres(mem, "w2b", buf * 4096, buf * 4096 + 4096),
                               PS(pbk[nh]))
                for j in range(2):
                    if gi == NKG1 - 1:
                        tt = 2 * e_ + j
                        for nh in range(2):
                            P.add("vector", lambda e, tt=tt, nh=nh, pb_=4 + j + 2 * nh: e.tensor_copy(
                                g[:, 1024 * tt + 512 * nh:1024 * tt + 512 * nh + 512], pk[pb_][:, 0:512]),
                                reads=PS(4 + j + 2 * nh),
                                writes=sres(mem, "g", 1024 * tt + 512 * nh, 1024 * tt + 512 * nh + 512))

    def emit_gates():
        for t in range(NT):
            for k in range(8):
                MM(XnT[:, 2048 * k + 128 * t:2048 * k + 128 * t + 128],
                   centT[:, 8 * k:8 * k + 8], pk[2][:, 0:8], k == 0, k == 7,
                   sres(mem, "XnT", 2048 * k + 128 * t, 2048 * k + 128 * t + 128) +
                   sres(mem, "centT"), PS(2))
            gA = gat.ap()[:, 8 * t:8 * t + 8]
            gF = gfin.ap()[:, 8 * t:8 * t + 8]
            gAr = sres(mem, "gat", 8 * t, 8 * t + 8)
            gFr = sres(mem, "gfin", 8 * t, 8 * t + 8)
            sm = smax.ap()
            smr = sres(mem, "smax")
            P.add("scalar", lambda e, gA=gA: e.activation(gA, pk[2][:, 0:8], AF.Sigmoid),
                  reads=PS(2), writes=gAr)
            P.add("vector", lambda e, gA=gA, sm=sm: e.reduce_max(sm[:, 0:1], gA, axis=AX.X),
                  reads=gAr, writes=smr)
            P.add("vector", lambda e, gA=gA, gF=gF, sm=sm: e.tensor_scalar(
                gF, gA, sm[:, 0:1], None, op0=AluOpType.is_equal), reads=gAr + smr, writes=gFr)
            P.add("vector", lambda e, gA=gA, gF=gF: e.scalar_tensor_tensor(
                gF, gF, -1e30, gA, op0=AluOpType.mult, op1=AluOpType.add), reads=gAr + gFr, writes=gFr)
            P.add("vector", lambda e, gF=gF, sm=sm: e.reduce_max(sm[:, 1:2], gF, axis=AX.X),
                  reads=gFr, writes=smr)
            P.add("vector", lambda e, gA=gA, gF=gF, sm=sm: e.tensor_scalar(
                gF, gA, sm[:, 1:2], None, op0=AluOpType.is_ge), reads=gAr + smr, writes=gFr)
            P.add("vector", lambda e, sm=sm: e.tensor_tensor(
                sm[:, 2:3], sm[:, 0:1], sm[:, 1:2], op=AluOpType.add), reads=smr, writes=smr)
            P.add("vector", lambda e, sm=sm: e.reciprocal(sm[:, 3:4], sm[:, 2:3]),
                  reads=smr, writes=smr)
            P.add("vector", lambda e, gA=gA, gF=gF, sm=sm: e.scalar_tensor_tensor(
                gF, gA, sm[:, 3:4], gF, op0=AluOpType.mult, op1=AluOpType.mult),
                reads=gAr + smr + gFr, writes=gFr)

    def emit_moe():
        for e_ in range(NE):
            for q in range(4):
                dma("gpsimd",
                    W2sb.ap().rearrange("p (k m) -> p k m", k=32)[:, 8 * q:8 * q + 8, :],
                    din["L1_W2"][e_][1024 * q:1024 * q + 1024, :].rearrange("(k p) m -> p k m", p=128),
                    sres(mem, "W2sb", 8192 * q, 8192 * q + 8192),
                    sres(mem, "W2sb", 8192 * q, 8192 * q + 8192), tag=f"moeW2.{e_}")
            b2b = e_ % 2
            dma("gpsimd", b2row[0:1, 1024 * b2b:1024 * b2b + 1024].rearrange("a b -> a b"),
                din["L1_b2"][e_:e_ + 1, :],
                sres(mem, "b2row", 1024 * b2b, 1024 * b2b + 1024),
                sres(mem, "b2row", 1024 * b2b, 1024 * b2b + 1024))
            for grp in range(4):
                tok0 = 512 * grp
                for mg in range(8):
                    buf = mg % 2
                    dma("gpsimd",
                        w1mb.ap().rearrange("p (b k m) -> p b k m", b=2, k=8)[:, buf, :, :],
                        din["L1_W1"][e_][:, 512 * mg:512 * mg + 512].rearrange("(k p) m -> p k m", p=128),
                        sres(mem, "w1mb", buf * 4096, buf * 4096 + 4096),
                        sres(mem, "w1mb", buf * 4096, buf * 4096 + 4096), tag=f"moeW1.{e_}.{grp}")
                    for mc in range(4):
                        mct = 4 * mg + mc
                        bnk = mct % 2
                        for k in range(8):
                            MM(w1mb.ap()[:, 4096 * buf + 512 * k + 128 * mc:4096 * buf + 512 * k + 128 * mc + 128],
                               XnT[:, 2048 * k + tok0:2048 * k + tok0 + 512],
                               pk[bnk][:, 0:512], k == 0, k == 7,
                               sres(mem, "w1mb", buf * 4096, buf * 4096 + 4096) +
                               sres(mem, "XnT", 2048 * k + tok0, 2048 * k + tok0 + 512),
                               PS(bnk))
                        P.add("scalar", lambda e, b=bnk, mct=mct, e_=e_: e.activation(
                            hTm[:, 512 * mct:512 * mct + 512], pk[b][:, 0:512], AF.Gelu,
                            bias=b1sb[:, 32 * e_ + mct:32 * e_ + mct + 1]),
                            reads=PS(bnk) + sres(mem, "b1sb"),
                            writes=sres(mem, "hTm", 512 * mct, 512 * mct + 512))
                for j in range(4):
                    for nh in range(2):
                        MM(onesrow[0:1, 0:128],
                           b2row[0:1, 1024 * b2b + 512 * nh:1024 * b2b + 512 * nh + 512],
                           pk[2 * j + nh][:, 0:512], True, False,
                           sres(mem, "onesrow") + sres(mem, "b2row", 1024 * b2b, 1024 * b2b + 1024),
                           PS(2 * j + nh))
                for kc in range(32):
                    for j in range(4):
                        for nh in range(2):
                            MM(hTm[:, 512 * kc + 128 * j:512 * kc + 128 * j + 128],
                               W2sb.ap()[:, 1024 * kc + 512 * nh:1024 * kc + 512 * nh + 512],
                               pk[2 * j + nh][:, 0:512], False, kc == 31,
                               sres(mem, "hTm", 512 * kc, 512 * kc + 512) +
                               sres(mem, "W2sb", 1024 * kc, 1024 * kc + 1024),
                               PS(2 * j + nh))
                for j in range(4):
                    tt = 4 * grp + j
                    for nh in range(2):
                        gap = g.ap()[:, 1024 * tt + 512 * nh:1024 * tt + 512 * nh + 512]
                        gr = sres(mem, "g", 1024 * tt + 512 * nh, 1024 * tt + 512 * nh + 512)
                        gsc = gfin.ap()[:, 8 * tt + e_:8 * tt + e_ + 1]
                        gscr = sres(mem, "gfin", 8 * tt, 8 * tt + 8)
                        if e_ == 0:
                            P.add("vector", lambda e, pbk=2 * j + nh, gap=gap, gsc=gsc: e.tensor_scalar(
                                gap, pk[pbk][:, 0:512], gsc, None, op0=AluOpType.mult),
                                reads=PS(2 * j + nh) + gscr, writes=gr)
                        else:
                            P.add("vector", lambda e, pbk=2 * j + nh, gap=gap, gsc=gsc: e.scalar_tensor_tensor(
                                gap, pk[pbk][:, 0:512], gsc, gap, op0=AluOpType.mult, op1=AluOpType.add),
                                reads=PS(2 * j + nh) + gscr + gr, writes=gr)

    def emit_attn(L):
        sc_ = 0.125 if L == 0 else 1.0
        QTz = [QTz0, QTz1]
        KTA = [KTA0, KTA1]
        MA = [MA0, MA1]
        r_ = lambda nm, a=None, b=None: mem.res(nm, a, b)

        def attn_init():
            for hh in (0, 1):
                ob = 64 * (1 - hh)
                tts = [(f"QTz{hh}", QTz[hh]), (f"KTA{hh}", KTA[hh])]
                if L == 1:
                    tts.append((f"MA{hh}", MA[hh]))
                for nm, tt in tts:
                    P.add("vector", lambda e, tt=tt, ob=ob: e.memset(
                        tt.ap()[ob:ob + 64, :], 0.0), (), r_(nm))

        def pair_weights(p):
            qoff = 128 * p
            koff = 128 * p if L == 0 else 1024 + 128 * p
            voff = 128 * p if L == 0 else 2048 + 128 * p
            srcs = [("wqb", (din["L0_Wq"] if L == 0 else din["L1_wqkv"]), qoff),
                    ("wkb", (din["L0_Wk"] if L == 0 else din["L1_wqkv"]), koff),
                    ("wvb", (din["L0_Wv"] if L == 0 else din["L1_wqkv"]), voff)]
            for nm, src, off in srcs:
                dma("gpsimd",
                    mem.tensors[nm][0].ap().rearrange("p (k m) -> p k m", k=8),
                    src[:, off:off + 128].rearrange("(k p) m -> p k m", p=128),
                    sres(mem, nm), sres(mem, nm), tag=f"wqkv{L}.{p}")

        def proj_qk(p):
            bq_col = (16 if L else 0) + p
            bk_col = (24 if L else 8) + p
            for which, wbn, bcol, dsts in (("q", "wqb", bq_col, QTz),
                                           ("k", "wkb", bk_col, KTA)):
                wb = mem.tensors[wbn][0]
                for n in range(4):
                    bnk = n % 2
                    for k in range(8):
                        MM(wb.ap()[:, 128 * k:128 * k + 128],
                           XnT[:, 2048 * k + 512 * n:2048 * k + 512 * n + 512],
                           pk[bnk][:, 0:512], k == 0, k == 7,
                           sres(mem, wbn) +
                           sres(mem, "XnT", 2048 * k + 512 * n, 2048 * k + 512 * n + 512),
                           PS(bnk))
                    for hh in (0, 1):
                        dn = (f"QTz{hh}" if which == "q" else f"KTA{hh}")
                        P.add("vector", lambda e, d=dsts[hh], hh=hh, n=n, bnk=bnk, bcol=bcol:
                              e.tensor_scalar(
                                  d.ap()[64 * hh:64 * hh + 64, 512 * n:512 * n + 512],
                                  pk[bnk][64 * hh:64 * hh + 64, 0:512],
                                  bqk[64 * hh:64 * hh + 64, bcol:bcol + 1], None,
                                  op0=AluOpType.add),
                              reads=PS(bnk) + sres(mem, "bqk"),
                              writes=r_(dn, 512 * n, 512 * n + 512))
            if L == 1:
                for hh in (0, 1):
                    P.add("vector", lambda e, hh=hh: e.tensor_scalar(
                        MA[hh].ap()[64 * hh:64 * hh + 64, :],
                        QTz[hh].ap()[64 * hh:64 * hh + 64, :],
                        -2.0, None, op0=AluOpType.mult),
                        reads=r_(f"QTz{hh}"), writes=r_(f"MA{hh}"))

        def proj_v(p):
            for t in range(NT):
                bnk = 2 + t % 2
                for k in range(8):
                    MM(XnT[:, 2048 * k + 128 * t:2048 * k + 128 * t + 128],
                       wvb.ap()[:, 128 * k:128 * k + 128],
                       pk[bnk][:, 0:128], k == 0, k == 7,
                       sres(mem, "wvb") + sres(mem, "XnT", 2048 * k + 128 * t, 2048 * k + 128 * t + 128),
                       PS(bnk))
                P.add("vector", lambda e, t=t, bnk=bnk: e.tensor_copy(
                    Vsb[:, 128 * t:128 * t + 128], pk[bnk][:, 0:128]),
                    reads=PS(bnk), writes=sres(mem, "Vsb", 128 * t, 128 * t + 128))

        def aug_build(p):
            for side, srcs, sel, seed, dsts in (
                    ("q", QTz, selq, seedq, MA), ("k", KTA, selk, seedk, KTA)):
                seln = "selq" if side == "q" else "selk"
                seedn = "seedq" if side == "q" else "seedk"
                for hh in (0, 1):
                    hb = 64 * hh
                    P.add("scalar", lambda e, s=srcs[hh], hb=hb: e.activation(
                        sqsb.ap()[hb:hb + 64, :], s.ap()[hb:hb + 64, :], AF.Square),
                        reads=r_(f"QTz{hh}" if side == "q" else f"KTA{hh}"),
                        writes=r_("sqsb"))
                for hh in (0, 1):
                    ob = 64 * (1 - hh)
                    dn = f"MA{hh}" if side == "q" else f"KTA{hh}"
                    for n in range(4):
                        MM(seed[0:1, :], onesrow[0:1, 0:512], pk[2][ob:ob + 2, 0:512],
                           True, False, sres(mem, seedn) + sres(mem, "onesrow"), PS(2))
                        MM(sel[:, 2 * hh:2 * hh + 2], sqsb.ap()[:, 512 * n:512 * n + 512],
                           pk[2][ob:ob + 2, 0:512], False, True,
                           sres(mem, seln) + r_("sqsb"), PS(2))
                        P.add("vector", lambda e, d=dsts[hh], ob=ob, n=n: e.tensor_copy(
                            d.ap()[ob:ob + 2, 512 * n:512 * n + 512], pk[2][ob:ob + 2, 0:512]),
                            reads=PS(2), writes=r_(dn))

        def scores_stage(p, t):
            Lrow = 128 * (t + 1)
            par = t % 2
            nq = (Lrow + 511) // 512
            for cq in range(nq):
              for hh in (0, 1):
                Eb = Ebuf[(hh, par)]
                Enm = f"E{hh}{par}"
                sb_ = 8 * (2 * par + hh)
                snm = f"scor{hh}"
                c0 = 512 * cq
                cw = min(512, Lrow - c0)
                if L == 0:
                    bnk = 2 * hh + cq % 2
                    MM(QTz[hh].ap()[:, 128 * t:128 * t + 128],
                       KTA[hh].ap()[:, c0:c0 + cw],
                       pk[bnk][:, 0:cw], True, True,
                       r_(f"QTz{hh}", 128 * t, 128 * t + 128) + r_(f"KTA{hh}", c0, c0 + cw),
                       PS(bnk))
                    if c0 + cw == Lrow:
                        P.add("vector", lambda e, bnk=bnk, o=cw - 128: e.tensor_tensor(
                            pk[bnk][:, o:o + 128], pk[bnk][:, o:o + 128], mask0[:], op=AluOpType.add),
                            reads=PS(bnk) + sres(mem, "mask0"), writes=PS(bnk))
                    P.add("scalar", lambda e, bnk=bnk, hh=hh, c0=c0, cw=cw: e.copy(
                        scor[hh].ap()[:, c0:c0 + cw], pk[bnk][:, 0:cw]),
                        reads=PS(bnk), writes=sres(mem, snm, c0, c0 + cw))
                else:
                    bd = 2 * hh
                    bn = 2 * hh + 1
                    MM(QTz[hh].ap()[:, 128 * t:128 * t + 128], KTA[hh].ap()[:, c0:c0 + cw],
                       pk[bd][:, 0:cw], True, True,
                       r_(f"QTz{hh}", 128 * t, 128 * t + 128) + r_(f"KTA{hh}", c0, c0 + cw),
                       PS(bd))
                    MM(MA[hh].ap()[:, 128 * t:128 * t + 128], KTA[hh].ap()[:, c0:c0 + cw],
                       pk[bn][:, 0:cw], True, True,
                       r_(f"MA{hh}", 128 * t, 128 * t + 128) + r_(f"KTA{hh}", c0, c0 + cw),
                       PS(bn))
                    P.add("scalar", lambda e, bd=bd, hh=hh, c0=c0, cw=cw: e.activation(
                        scor[hh].ap()[:, c0:c0 + cw], pk[bd][:, 0:cw], AF.Square),
                        reads=PS(bd), writes=sres(mem, snm, c0, c0 + cw))
                    P.add("scalar", lambda e, bn=bn, hh=hh, cw=cw: e.activation(
                        rdb[hh].ap()[:, 0:cw], pk[bn][:, 0:cw], AF.Ln),
                        reads=PS(bn), writes=sres(mem, f"rd{hh}"))
                    P.add("scalar", lambda e, hh=hh, cw=cw: e.activation(
                        rdb[hh].ap()[:, 0:cw], rdb[hh].ap()[:, 0:cw], AF.Exp, scale=-1.0),
                        reads=sres(mem, f"rd{hh}"), writes=sres(mem, f"rd{hh}"))
                    P.add("vector", lambda e, hh=hh, c0=c0, cw=cw: e.tensor_tensor(
                        scor[hh].ap()[:, c0:c0 + cw], scor[hh].ap()[:, c0:c0 + cw],
                        rdb[hh].ap()[:, 0:cw], op=AluOpType.mult),
                        reads=sres(mem, snm, c0, c0 + cw) + sres(mem, f"rd{hh}"),
                        writes=sres(mem, snm, c0, c0 + cw))
                    if c0 + cw == Lrow:
                        P.add("vector", lambda e, hh=hh, o=Lrow - 128: e.tensor_tensor(
                            scor[hh].ap()[:, o:o + 128], scor[hh].ap()[:, o:o + 128],
                            mask1[:], op=AluOpType.add),
                            reads=sres(mem, snm, Lrow - 128, Lrow) + sres(mem, "mask1"),
                            writes=sres(mem, snm, Lrow - 128, Lrow))
            for hh in (0, 1):
                sb_ = 8 * (2 * par + hh)
                snm = f"scor{hh}"
                P.add("vector", lambda e, hh=hh, Lr=Lrow, sb_=sb_: e.reduce_max(
                    smax[:, sb_:sb_ + 1], scor[hh].ap()[:, 0:Lr], axis=AX.X, negate=True),
                    reads=sres(mem, snm, 0, Lrow), writes=sres(mem, "smax"))
            for hh in (0, 1):
                sb_ = 8 * (2 * par + hh)
                P.add("vector", lambda e, sb_=sb_: e.tensor_scalar(
                    smax[:, sb_ + 1:sb_ + 2], smax[:, sb_:sb_ + 1], sc_, None, op0=AluOpType.mult),
                    reads=sres(mem, "smax"), writes=sres(mem, "smax"))
            for hh in (0, 1):
                sb_ = 8 * (2 * par + hh)
                snm = f"scor{hh}"
                Eb = Ebuf[(hh, par)]
                Enm = f"E{hh}{par}"
                P.add("scalar", lambda e, hh=hh, Eb=Eb, Lr=Lrow, sb_=sb_, Enm=Enm: e.activation(
                    Eb.ap()[:, 0:Lr], scor[hh].ap()[:, 0:Lr], AF.Exp,
                    bias=smax[:, sb_ + 1:sb_ + 2], scale=sc_,
                    accum_out=smax[:, sb_ + 2:sb_ + 3]),
                    reads=sres(mem, snm, 0, Lrow) + sres(mem, "smax"),
                    writes=sres(mem, Enm, 0, Lrow) + sres(mem, "smax"))
            for hh in (0, 1):
                sb_ = 8 * (2 * par + hh)
                P.add("vector", lambda e, sb_=sb_: e.reciprocal(
                    smax[:, sb_ + 3:sb_ + 4], smax[:, sb_ + 2:sb_ + 3]),
                    reads=sres(mem, "smax"), writes=sres(mem, "smax"))
            for hh in (0, 1):
                sb_ = 8 * (2 * par + hh)
                Eb = Ebuf[(hh, par)]
                Enm = f"E{hh}{par}"
                P.add("vector", lambda e, Eb=Eb, Enm=Enm, Lr=Lrow, sb_=sb_: e.tensor_scalar(
                    Eb.ap()[:, 0:Lr], Eb.ap()[:, 0:Lr], smax[:, sb_ + 3:sb_ + 4],
                    None, op0=AluOpType.mult),
                    reads=sres(mem, Enm, 0, Lrow) + sres(mem, "smax"),
                    writes=sres(mem, Enm, 0, Lrow))

        def txav_stage(p, t):
            par = t % 2
            ntile = t + 1
            for grp in range((ntile + 7) // 8):
                ntg = min(8, ntile - 8 * grp)
                for hh in (0, 1):
                    Eb = Ebuf[(hh, par)]
                    Enm = f"E{hh}{par}"
                    bnk = 6 + tctr[0] % 2
                    tctr[0] += 1
                    for j in range(ntg):
                        kt = 8 * grp + j
                        P.add("tensor", lambda e, Eb=Eb, kt=kt, j=j, bnk=bnk: e.transpose(
                            pb[bnk][:, 128 * j:128 * j + 128],
                            Eb.ap()[:, 128 * kt:128 * kt + 128], identb[:]),
                            reads=sres(mem, Enm, 128 * kt, 128 * kt + 128) + sres(mem, "identb"),
                            writes=PS(bnk))
                    P.add("vector", lambda e, hh=hh, grp=grp, bnk=bnk, ntg=ntg: e.tensor_copy(
                        ETb[hh].ap()[:, 1024 * grp:1024 * grp + 128 * ntg], pb[bnk][:, 0:128 * ntg]),
                        reads=PS(bnk),
                        writes=sres(mem, f"ET{hh}", 1024 * grp, 1024 * grp + 128 * ntg))
            for hh in (0, 1):
                avb = 4 + hh
                for kt in range(ntile):
                    MM(Vsb[:, 128 * kt + 64 * hh:128 * kt + 64 * hh + 64],
                       ETb[hh].ap()[:, 128 * kt:128 * kt + 128],
                       pk[avb][0:64, 0:128], kt == 0, kt == ntile - 1,
                       sres(mem, "Vsb", 128 * kt, 128 * kt + 128) +
                       sres(mem, f"ET{hh}", 128 * kt, 128 * kt + 128),
                       PS(avb))
            for hh in (0, 1):
                avb = 4 + hh
                P.add("vector", lambda e, hh=hh, p=p, t=t, avb=avb: e.tensor_copy(
                    OTall[64 * hh:64 * hh + 64, 2048 * p + 128 * t:2048 * p + 128 * t + 128],
                    pk[avb][0:64, 0:128]),
                    reads=PS(avb),
                    writes=sres(mem, "OTall", 2048 * p + 128 * t, 2048 * p + 128 * t + 128))

        tctr = [0]
        attn_init()
        pair_weights(0)
        proj_qk(0)
        proj_v(0)
        if L == 1:
            aug_build(0)
        for p in range(NPAIR):
            for t in range(NT):
                if t >= 2:
                    txav_stage(p, t - 2)
                scores_stage(p, t)
                if t == 2 and p + 1 < NPAIR:
                    pair_weights(p + 1)
            if p + 1 < NPAIR:
                proj_qk(p + 1)
            for t in (NT - 2, NT - 1):
                txav_stage(p, t)
            if p + 1 < NPAIR:
                proj_v(p + 1)
                if L == 1:
                    aug_build(p + 1)

        wo_src = din["L0_Wo"] if L == 0 else din["L1_wout"]
        for q in range(2):
            dma("gpsimd",
                Wosb.ap().rearrange("p (k m) -> p k m", k=8)[:, 4 * q:4 * q + 4, :],
                wo_src[512 * q:512 * q + 512, :].rearrange("(k p) m -> p k m", p=128),
                sres(mem, "Wosb", 4096 * q, 4096 * q + 4096),
                sres(mem, "Wosb", 4096 * q, 4096 * q + 4096), tag=f"wo{L}")
        bv_src = din["L0_bv"] if L == 0 else din["L1_bqkv"][2048:3072]
        bo_src = din["L0_bo"] if L == 0 else din["L1_bout"]
        dma("sync", btmp[0:8, :], bv_src.rearrange("(a b) -> a b", b=128),
            sres(mem, "btmp"), sres(mem, "btmp"))
        P.add("tensor", lambda e: e.transpose(pk[7][0:128, 0:8], btmp[0:8, :], identf[0:8, 0:8]),
              reads=sres(mem, "btmp") + sres(mem, "identf"), writes=PS(7))
        P.add("vector", lambda e: e.tensor_copy(bvc[:], pk[7][:, 0:8]),
              reads=PS(7), writes=sres(mem, "bvc"))
        dma("gpsimd", borow[0:1, :], bo_src.rearrange("(a b) -> a b", a=1),
            sres(mem, "borow"), sres(mem, "borow"))
        for nh in range(2):
            MM(onesrow[0:1, 0:1], borow[0:1, 512 * nh:512 * nh + 512], pk[6][0:1, 0:512] if nh == 0 else pk[7][0:1, 0:512],
               True, False, sres(mem, "onesrow") + sres(mem, "borow"), PS(6 + nh))
            for k in range(8):
                MM(bvc[:, k:k + 1], Wosb.ap()[:, 1024 * k + 512 * nh:1024 * k + 512 * nh + 512],
                   pk[6 + nh][0:1, 0:512], False, k == 7,
                   sres(mem, "bvc") + sres(mem, "Wosb"), PS(6 + nh))
            P.add("vector", lambda e, nh=nh: e.tensor_copy(
                bprow[0:1, 512 * nh:512 * nh + 512], (pk[6 + nh])[0:1, 0:512]),
                reads=PS(6 + nh), writes=sres(mem, "bprow"))
        for nh in range(2):
            MM(onesrow[0:1, 0:128], bprow[0:1, 512 * nh:512 * nh + 512], pk[6 + nh][:, 0:512],
               True, True, sres(mem, "onesrow") + sres(mem, "bprow"), PS(6 + nh))
            P.add("vector", lambda e, nh=nh: e.tensor_copy(
                bobc[:, 512 * nh:512 * nh + 512], pk[6 + nh][:, 0:512]),
                reads=PS(6 + nh), writes=sres(mem, "bobc", 512 * nh, 512 * nh + 512))

        xsrc = X0 if L == 0 else dx1
        xdst = dx1 if L == 0 else out
        xres_r = (lambda t: (f"dx0c.{t}",)) if L == 0 else (lambda t: (f"dx1.{t}",))
        xres_w = (lambda t: (f"dx1.{t}",)) if L == 0 else (lambda t: (f"dout.{t}",))
        for t in range(NT):
            xb = xbuf.ap()[:, (t % 2) * D:(t % 2) * D + D]
            xrs = sres(mem, "xbuf", (t % 2) * D, (t % 2) * D + D)
            dma("sync", xb, xsrc[128 * t:128 * t + 128, :], xres_r(t), xrs, tag=f"cmbx{L}.{t}")
            for nh in range(2):
                bnk = nh
                for k in range(8):
                    MM(OTall[:, 2048 * k + 128 * t:2048 * k + 128 * t + 128],
                       Wosb.ap()[:, 1024 * k + 512 * nh:1024 * k + 512 * nh + 512],
                       pk[bnk][:, 0:512], k == 0, k == 7,
                       sres(mem, "OTall", 2048 * k + 128 * t, 2048 * k + 128 * t + 128) +
                       sres(mem, "Wosb"),
                       PS(bnk))
            P.add("scalar", lambda e, t=t: e.activation(gs[:], g[:, 1024 * t:1024 * t + 1024], AF.Silu),
                  reads=sres(mem, "g", 1024 * t, 1024 * t + 1024), writes=sres(mem, "gs"))
            for nh in range(2):
                P.add("vector", lambda e, nh=nh: e.tensor_tensor(
                    t1[:, 512 * nh:512 * nh + 512], pk[nh][:, 0:512],
                    bobc[:, 512 * nh:512 * nh + 512], op=AluOpType.add),
                    reads=PS(nh) + sres(mem, "bobc", 512 * nh, 512 * nh + 512),
                    writes=sres(mem, "t1", 512 * nh, 512 * nh + 512))
            P.add("vector", lambda e: e.tensor_tensor(t1[:], t1[:], gs[:], op=AluOpType.mult),
                  reads=sres(mem, "t1") + sres(mem, "gs"), writes=sres(mem, "t1"))
            P.add("vector", lambda e, xb=xb: e.tensor_tensor(xb, t1[:], xb, op=AluOpType.add),
                  reads=sres(mem, "t1") + xrs, writes=xrs)
            dma("sync", xdst[128 * t:128 * t + 128, :], xb, xrs, xres_w(t), tag=f"cmbo{L}.{t}")

    if 0 not in layers:
        for t in range(NT):
            xb = xbuf.ap()[:, (t % 2) * D:(t % 2) * D + D]
            xrs = sres(mem, "xbuf", (t % 2) * D, (t % 2) * D + D)
            dma("sync", xb, X0[128 * t:128 * t + 128, :], (f"dx0.{t}",), xrs)
            dma("sync", dx1[128 * t:128 * t + 128, :], xb, xrs, (f"dx1.{t}",))
    if 0 in layers:
        emit_ln(0)
        emit_ffn0()
        emit_attn(0)
    if 1 in layers:
        emit_ln(1)
        emit_moe()
        emit_attn(1)
    if 1 not in layers:
        for t in range(NT):
            xb = xbuf.ap()[:, (t % 2) * D:(t % 2) * D + D]
            xrs = sres(mem, "xbuf", (t % 2) * D, (t % 2) * D + D)
            dma("sync", xb, dx1[128 * t:128 * t + 128, :], (f"dx1.{t}",), xrs)
            dma("sync", out[128 * t:128 * t + 128, :], xb, xrs, (f"dout.{t}",))

    for tname, bufname, cols, dt in taps:
        tsr = sres(mem, bufname, 0, cols)
        tb, _, _ = mem.tensors[bufname]
        npart = tb.shape[0]
        tout = nc.dram_tensor("tap_" + tname, [npart, cols], dt, kind="ExternalOutput").ap()
        dma("sync", tout, tb.ap()[:, 0:cols], tsr, (f"tap.{tname}",))

    P.finalize(nc, block, sems)
    stack.close()
    return nc


def make_consts():
    tri0 = np.triu(np.ones((128, 128), np.float32), k=1)
    selq = np.zeros((128, 4), np.float32)
    selq[0:64, 1] = 1.0
    selq[64:128, 3] = 1.0
    selk = np.zeros((128, 4), np.float32)
    selk[0:64, 0] = 1.0
    selk[64:128, 2] = 1.0
    return {
        "c_ident_bf": np.eye(128).astype(BF16NP),
        "c_ident_f32": np.eye(128).astype(np.float32),
        "c_mask0": (tri0 * -1e30).astype(np.float32),
        "c_mask1": (tri0 * -1e9).astype(np.float32),
        "c_ones_row": np.ones((1, 512), BF16NP),
        "c_seedq": np.array([[1.0, 1e-6]], BF16NP),
        "c_seedk": np.array([[0.0, 1.0]], BF16NP),
        "c_selq": selq.astype(BF16NP),
        "c_selk": selk.astype(BF16NP),
        "c_idx": (1000.0 + np.arange(8, dtype=np.float32))[None, :].repeat(128, 0),
    }


def make_in_maps(inputs, layers=(0, 1)):
    consts = make_consts()
    base = dict(consts)
    f32c = lambda a: np.ascontiguousarray(np.asarray(a), dtype=np.float32)
    if 0 in layers:
        base.update({
            "L0_Wq": f32c(inputs["L0_attn_Wq"]), "L0_Wk": f32c(inputs["L0_attn_Wk"]),
            "L0_Wv": f32c(inputs["L0_attn_Wv"]), "L0_Wo": f32c(inputs["L0_attn_Wo"]),
            "L0_bq": f32c(inputs["L0_attn_bq"]), "L0_bk": f32c(inputs["L0_attn_bk"]),
            "L0_bv": f32c(inputs["L0_attn_bv"]), "L0_bo": f32c(inputs["L0_attn_bo"]),
            "L0_w1": f32c(inputs["L0_ffn_w1"]), "L0_w3": f32c(inputs["L0_ffn_w3"]),
            "L0_w2": f32c(inputs["L0_ffn_w2"]),
        })
    if 1 in layers:
        base.update({
            "L1_wqkv": f32c(inputs["L1_attn_w_qkv"]), "L1_bqkv": f32c(inputs["L1_attn_b_qkv"]),
            "L1_wout": f32c(inputs["L1_attn_w_out"]), "L1_bout": f32c(inputs["L1_attn_b_out"]),
            "L1_cent": f32c(inputs["L1_ffn_centroids"]), "L1_W1": f32c(inputs["L1_ffn_W1"]),
            "L1_b1": f32c(inputs["L1_ffn_b1"]), "L1_W2": f32c(inputs["L1_ffn_W2"]),
            "L1_b2": f32c(inputs["L1_ffn_b2"]),
        })
    X = f32c(inputs["L0_attn_X"])
    maps = []
    for c in range(8):
        m = dict(base)
        m["x0"] = np.ascontiguousarray(X[c])
        maps.append(m)
    return maps


def kernel(**inputs):
    nc = build_nc(layers=(0, 1))
    maps = make_in_maps(inputs, layers=(0, 1))
    res = run_bass_kernel_spmd(nc, maps, core_ids=list(range(8)))
    return np.stack([np.asarray(r["out"]) for r in res.results], axis=0)


# version 08
# speedup vs baseline: 1.1085x; 1.1085x over previous
import numpy as np
import ml_dtypes
import concourse.bass as bass
import concourse.mybir as mybir
from concourse.bass_utils import run_bass_kernel_spmd
from concourse.alu_op_type import AluOpType

F32 = mybir.dt.float32
BF16 = mybir.dt.bfloat16
AF = mybir.ActivationFunctionType
AX = mybir.AxisListType
BF16NP = ml_dtypes.bfloat16

S = 2048
D = 1024
NT = 16
H = 16
HD = 64
NPAIR = 8
DFF0 = 2816
NE = 8
DFF1 = 4096

ENGS = ("sync", "gpsimd", "tensor", "vector", "scalar")
DMA_ENGS = ("sync", "gpsimd")


class Prog:

    def __init__(self):
        self.eops = {e: [] for e in ENGS}
        self.lastw = {}
        self.readers = {}

    def add(self, eng, emit, reads=(), writes=(), tag=""):
        idx = len(self.eops[eng])
        deps = {}

        def need(e2, i2):
            if e2 == eng:
                if eng == "tensor":
                    return
            if deps.get(e2, -1) < i2:
                deps[e2] = i2

        for r in reads:
            lw = self.lastw.get(r)
            if lw is not None:
                need(*lw)
        for w in writes:
            lw = self.lastw.get(w)
            if lw is not None:
                need(*lw)
            for e2, i2 in self.readers.get(w, {}).items():
                need(e2, i2)
        self.eops[eng].append([emit, deps, eng in DMA_ENGS, tag])
        for r in reads:
            self.readers.setdefault(r, {})
            if self.readers[r].get(eng, -1) < idx:
                self.readers[r][eng] = idx
        for w in writes:
            self.lastw[w] = (eng, idx)
            self.readers[w] = {eng: idx}

    def finalize(self, nc, block, sems):
        marked = {e: set() for e in ENGS}
        for e in ENGS:
            for op in self.eops[e]:
                for e2, i2 in op[1].items():
                    marked[e2].add(i2)
        semval = {}
        for e in ENGS:
            if e in DMA_ENGS:
                semval[e] = [16 * (i + 1) for i in range(len(self.eops[e]))]
            else:
                c = 0
                vals = []
                for i in range(len(self.eops[e])):
                    if i in marked[e]:
                        c += 1
                    vals.append(c)
                semval[e] = vals
        totals = {e: (semval[e][-1] if self.eops[e] else 0) for e in ENGS}

        def body(ename):
            ops = self.eops[ename]
            mk = marked[ename]

            def f(eng):
                water = {e: 0 for e in ENGS}
                for i, (emit, deps, is_dma, tag) in enumerate(ops):
                    for e2, i2 in sorted(deps.items()):
                        v = semval[e2][i2]
                        if water[e2] < v:
                            eng.wait_ge(sems[e2], v)
                            water[e2] = v
                    inst = emit(eng)
                    if is_dma:
                        inst.then_inc(sems[ename], 16)
                    elif i in mk:
                        inst.then_inc(sems[ename], 1)
                if ename == "sync":
                    for e2 in ENGS:
                        if totals[e2] > 0 and e2 != "sync":
                            eng.wait_ge(sems[e2], totals[e2])
                    if totals["sync"] > 0:
                        eng.wait_ge(sems["sync"], totals["sync"])
            return f

        for e in ENGS:
            getattr(block, e)(body(e))


BLK = 2048


SBUF_BASE = 16544
SBUF_LIMIT = 229376


class Mem:
    def __init__(self, nc):
        self.nc = nc
        self.off = SBUF_BASE
        self.tensors = {}

    def at(self, name, shape, dtype, offset=None):
        nbytes = shape[-1] * (2 if dtype == BF16 else 4)
        if len(shape) > 2:
            for s in shape[1:-1]:
                nbytes *= s
        if offset is None:
            offset = self.off
            self.off = offset + ((nbytes + 31) // 32 * 32)
        t = self.nc.alloc_sbuf_tensor_at(name, shape, dtype, offset=offset)
        self.tensors[name] = (t, offset, nbytes)
        return t

    def res(self, name, col0=None, col1=None):
        t, off, nbytes = self.tensors[name]
        esz = 2 if t.dtype == BF16 else 4
        b0 = off if col0 is None else off + col0 * esz
        b1 = off + nbytes if col1 is None else off + col1 * esz
        return tuple(f"sb{b}" for b in range(b0 // BLK, (b1 - 1) // BLK + 1))


def sres(mem, name, c0=None, c1=None):
    return mem.res(name, c0, c1)


def build_nc(layers=(0, 1), taps=None):
    nc = bass.Bass("TRN2", target_bir_lowering=False)
    P = Prog()

    din = {}

    def dram_in(name, shape, dt=F32):
        din[name] = nc.dram_tensor(name, shape, dt, kind="ExternalInput").ap()
        return din[name]

    X0 = dram_in("x0", [S, D])
    out = nc.dram_tensor("out", [S, D], F32, kind="ExternalOutput").ap()
    dx1 = nc.dram_tensor("dx1", [S, D], F32, kind="Internal").ap()

    if 0 in layers:
        for nm, sh in [("Wq", [D, D]), ("Wk", [D, D]), ("Wv", [D, D]), ("Wo", [D, D]),
                       ("bq", [D]), ("bk", [D]), ("bv", [D]), ("bo", [D]),
                       ("w1", [NE, D, DFF0]), ("w3", [NE, D, DFF0]), ("w2", [NE, DFF0, D])]:
            dram_in("L0_" + nm, sh)
    if 1 in layers:
        for nm, sh in [("wqkv", [D, 3 * D]), ("bqkv", [3 * D]), ("wout", [D, D]), ("bout", [D]),
                       ("cent", [NE, D]), ("W1", [NE, D, DFF1]), ("b1", [NE, DFF1]),
                       ("W2", [NE, DFF1, D]), ("b2", [NE, D])]:
            dram_in("L1_" + nm, sh)
    dram_in("c_ident_bf", [128, 128], BF16)
    dram_in("c_ident_f32", [128, 128], F32)
    dram_in("c_mask0", [128, 128], F32)
    dram_in("c_mask1", [128, 128], F32)
    dram_in("c_ones_row", [1, 512], BF16)
    dram_in("c_seedq", [1, 2], BF16)
    dram_in("c_seedk", [1, 2], BF16)
    dram_in("c_selq", [128, 4], BF16)
    dram_in("c_selk", [128, 4], BF16)
    dram_in("c_idx", [128, 8], F32)

    taps = taps or []

    from contextlib import ExitStack
    stack = ExitStack()
    mem = Mem(nc)

    identb = mem.at("identb", [128, 128], BF16)
    identf = mem.at("identf", [128, 128], F32)
    mask0 = mem.at("mask0", [128, 128], F32)
    mask1 = mem.at("mask1", [128, 128], F32)
    onesrow = mem.at("onesrow", [1, 512], BF16)
    seedq = mem.at("seedq", [1, 2], BF16)
    seedk = mem.at("seedk", [1, 2], BF16)
    selq = mem.at("selq", [128, 4], BF16)
    selk = mem.at("selk", [128, 4], BF16)
    ceps = mem.at("ceps", [128, 1], F32)
    cidx = mem.at("cidx", [128, 8], F32)
    gscr = mem.at("gscr", [128, 8], F32)
    bqk = mem.at("bqk", [128, 32], F32)
    b1sb = mem.at("b1sb", [128, 256], F32)
    btmp = mem.at("btmp", [32, 128], F32)
    bvc = mem.at("bvc", [128, 8], BF16)
    borow = mem.at("borow", [1, 1024], BF16)
    bprow = mem.at("bprow", [1, 1024], BF16)
    centb = mem.at("centb", [8, 1024], F32)
    centT = mem.at("centT", [128, 64], F32)
    mv = mem.at("mv", [128, 4], F32)
    bnst = mem.at("bnst", [128, 12], F32)
    smax = mem.at("smax", [128, 64], F32)
    gat = mem.at("gat", [128, 128], F32)
    gfin = mem.at("gfin", [128, 128], F32)
    xnb = mem.at("xnb", [128, 1024], BF16)
    mem.off = (mem.off + BLK - 1) // BLK * BLK
    XnT = mem.at("XnT", [128, 8 * S], BF16)
    g = mem.at("g", [128, NT * D], BF16)
    xbuf = mem.at("xbuf", [128, 2 * D], F32)
    arena0 = mem.off
    mem.off = arena0
    w1b = mem.at("w1b", [128, 2 * 4096], BF16)
    w3b = mem.at("w3b", [128, 2 * 4096], BF16)
    w2b = mem.at("w2b", [128, 2 * 4096], BF16)
    hT0 = mem.at("hT0", [128, 22 * 256], BF16)
    hs0 = mem.at("hs0", [128, 2 * 256], BF16)
    mem.off = arena0
    W2sb = mem.at("W2sb", [128, 32 * 1024], BF16)
    xcb = mem.at("xcb", [128, 1024], F32, offset=mem.tensors["W2sb"][1])
    xcT = mem.at("xcT", [128, 1024], F32, offset=mem.tensors["W2sb"][1] + 4096)
    hTm = mem.at("hTm", [128, 32 * 512], BF16)
    w1mb = mem.at("w1mb", [128, 2 * 4096], BF16)
    b2row = mem.at("b2row", [1, 2048], BF16)
    mem.off = arena0
    QTz0 = mem.at("QTz0", [128, S], BF16)
    QTz1 = mem.at("QTz1", [128, S], BF16)
    KTA0 = mem.at("KTA0", [128, S], BF16)
    KTA1 = mem.at("KTA1", [128, S], BF16)
    MA0 = mem.at("MA0", [128, S], BF16)
    MA1 = mem.at("MA1", [128, S], BF16)
    Vsb = mem.at("Vsb", [128, S], BF16)
    Wosb = mem.at("Wosb", [128, 8 * 1024], BF16, offset=mem.tensors["QTz0"][1])
    wqb = mem.at("wqb", [128, 1024], BF16)
    wkb = mem.at("wkb", [128, 1024], BF16)
    wvb = mem.at("wvb", [128, 1024], BF16)
    scor = {}
    Ebuf = {}
    ETb = {}
    sqd = {}
    rdb = {}
    for h in (0, 1):
        scor[(h, 0)] = mem.at(f"scor{h}", [128, S], F32)
        for par in (0, 1):
            Ebuf[(h, par)] = mem.at(f"E{h}{par}", [128, S], BF16)
        ETb[h] = mem.at(f"ET{h}", [128, S], BF16)
    OTall = mem.at("OTall", [128, 8 * S], BF16)
    scor[(0, 1)] = mem.at("scorX0", [128, S], F32)
    scor[(1, 1)] = mem.at("scorX1", [128, S], F32, offset=mem.tensors["xbuf"][1])
    gs = mem.at("gs", [128, 1024], F32, offset=mem.tensors["E00"][1])
    t1 = mem.at("t1", [128, 1024], F32, offset=mem.tensors["E01"][1])
    bobc = mem.at("bobc", [128, D], F32, offset=mem.tensors["E10"][1])
    rdb[0] = mem.at("rd0", [128, 512], F32, offset=mem.tensors["centb"][1])
    rdb[1] = mem.at("rd1", [128, 512], F32, offset=mem.tensors["centb"][1] + 2048)
    sqsb = mem.at("sqsb", [128, S], BF16, offset=mem.tensors["scor0"][1])
    assert mem.off <= SBUF_LIMIT, f"SBUF overflow: {mem.off}"

    pk = [stack.enter_context(nc.psum_tensor(f"pk{i}", [128, 512], F32)) for i in range(8)]
    pb = [t.bitcast(BF16) for t in pk]

    sems = {}
    for e in ENGS:
        sems[e] = stack.enter_context(nc.semaphore("sem_" + e))
    block = stack.enter_context(nc.Block())

    PS = lambda i: (f"ps{i}",)

    def dma(queue, outap, inap, reads, writes, tag=""):
        P.add(queue, lambda e, o=outap, i=inap: e.dma_start(out=o, in_=i),
              reads=reads, writes=writes, tag=tag)

    def MM(lhsT, rhs, outp, start, stop, reads, writes, tag=""):
        P.add("tensor",
              lambda e, a=lhsT, b=rhs, o=outp, s1=start, s2=stop:
                  e.matmul(o, a, b, start=s1, stop=s2),
              reads=reads, writes=writes, tag=tag)

    dma("sync", identb[:], din["c_ident_bf"], (), sres(mem, "identb"))
    dma("sync", identf[:], din["c_ident_f32"], (), sres(mem, "identf"))
    dma("sync", mask0[:], din["c_mask0"], (), sres(mem, "mask0"))
    dma("sync", mask1[:], din["c_mask1"], (), sres(mem, "mask1"))
    dma("sync", onesrow[:], din["c_ones_row"], (), sres(mem, "onesrow"))
    dma("sync", seedq[:], din["c_seedq"], (), sres(mem, "seedq"))
    dma("sync", seedk[:], din["c_seedk"], (), sres(mem, "seedk"))
    dma("sync", selq[:], din["c_selq"], (), sres(mem, "selq"))
    dma("sync", selk[:], din["c_selk"], (), sres(mem, "selk"))
    P.add("vector", lambda e: e.memset(ceps[:], 1e-5), (), sres(mem, "ceps"))
    dma("sync", cidx[:], din["c_idx"], (), sres(mem, "cidx"))

    def load_bias_cols(dsrc, dstap, dst_res, n):
        dma("sync", btmp[0:n, :], dsrc.rearrange("(a b) -> a b", b=128),
            sres(mem, "btmp"), sres(mem, "btmp"))
        P.add("tensor", lambda e, nn=n: e.transpose(pk[7][0:128, 0:nn], btmp[0:nn, :], identf[0:nn, 0:nn]),
              reads=sres(mem, "btmp") + sres(mem, "identf"), writes=PS(7))
        P.add("vector", lambda e, d=dstap, nn=n: e.tensor_copy(d, pk[7][:, 0:nn]),
              reads=PS(7), writes=dst_res)

    if 0 in layers:
        load_bias_cols(din["L0_bq"], bqk[:, 0:8], sres(mem, "bqk"), 8)
        load_bias_cols(din["L0_bk"], bqk[:, 8:16], sres(mem, "bqk"), 8)
    if 1 in layers:
        load_bias_cols(din["L1_bqkv"][0:1024], bqk[:, 16:24], sres(mem, "bqk"), 8)
        load_bias_cols(din["L1_bqkv"][1024:2048], bqk[:, 24:32], sres(mem, "bqk"), 8)
        for e_ in range(NE):
            load_bias_cols(din["L1_b1"][e_], b1sb[:, 32 * e_:32 * e_ + 32],
                           sres(mem, "b1sb"), 32)
        dma("sync", centb[:], din["L1_cent"], (), sres(mem, "centb"))
        for k in range(8):
            P.add("tensor", lambda e, k=k: e.transpose(pk[7][0:128, 0:8],
                                                       centb[:, 128 * k:128 * k + 128],
                                                       identf[0:8, 0:8]),
                  reads=sres(mem, "centb") + sres(mem, "identf"), writes=PS(7))
            P.add("vector", lambda e, k=k: e.tensor_copy(centT[:, 8 * k:8 * k + 8], pk[7][:, 0:8]),
                  reads=PS(7), writes=sres(mem, "centT"))

    def emit_ln(L):
        xsrc = X0 if L == 0 else dx1
        xres = (lambda t: (f"dx0.{t}",)) if L == 0 else (lambda t: (f"dx1.{t}",))
        for t in range(NT):
            xb = xbuf.ap()[:, (t % 2) * D:(t % 2) * D + D]
            xrs = sres(mem, "xbuf", (t % 2) * D, (t % 2) * D + D)
            dma("sync", xb, xsrc[128 * t:128 * t + 128, :], xres(t), xrs, tag=f"lnx{t}")
            P.add("vector", lambda e, xb=xb: e.bn_stats(bnst[:, 0:6], xb[:, 0:512]),
                  reads=xrs, writes=sres(mem, "bnst"))
            P.add("vector", lambda e, xb=xb: e.bn_stats(bnst[:, 6:12], xb[:, 512:1024]),
                  reads=xrs, writes=sres(mem, "bnst"))
            P.add("vector", lambda e: e.bn_aggr(mv[:, 0:2], bnst[:]),
                  reads=sres(mem, "bnst"), writes=sres(mem, "mv"))
            P.add("scalar", lambda e: e.activation(mv[:, 2:3], mv[:, 1:2], AF.Sqrt, bias=ceps[:]),
                  reads=sres(mem, "mv") + sres(mem, "ceps"), writes=sres(mem, "mv"))
            P.add("vector", lambda e: e.reciprocal(mv[:, 3:4], mv[:, 2:3]),
                  reads=sres(mem, "mv"), writes=sres(mem, "mv"))
            if L == 0:
                P.add("vector", lambda e, xb=xb: e.tensor_scalar(
                    xnb[:], xb, mv[:, 0:1], mv[:, 3:4], op0=AluOpType.subtract, op1=AluOpType.mult),
                    reads=xrs + sres(mem, "mv"), writes=sres(mem, "xnb"))
            else:
                P.add("vector", lambda e, xb=xb: e.tensor_scalar(
                    xcb[:], xb, mv[:, 0:1], None, op0=AluOpType.subtract),
                    reads=xrs + sres(mem, "mv"), writes=sres(mem, "xcb"))
                P.add("vector", lambda e: e.tensor_scalar(
                    xnb[:], xcb[:], mv[:, 3:4], None, op0=AluOpType.mult),
                    reads=sres(mem, "xcb") + sres(mem, "mv"), writes=sres(mem, "xnb"))
            bank = 6 + (t % 2)
            for c in range(8):
                MM0 = lambda e, c=c, bank=bank: e.transpose(
                    pb[bank][:, 128 * c:128 * c + 128], xnb[:, 128 * c:128 * c + 128], identb[:])
                P.add("tensor", MM0, reads=sres(mem, "xnb") + sres(mem, "identb"), writes=PS(bank))
            for c in range(8):
                P.add("vector", lambda e, c=c, t=t, bank=bank: e.tensor_copy(
                    XnT[:, 2048 * c + 128 * t:2048 * c + 128 * t + 128],
                    pb[bank][:, 128 * c:128 * c + 128]),
                    reads=PS(bank), writes=sres(mem, "XnT", 2048 * c + 128 * t, 2048 * c + 128 * t + 128))
            if L == 1:
                fb = 4 + (t % 2)
                for c in range(8):
                    P.add("tensor", lambda e, c=c, fb=fb: e.transpose(
                        pk[fb][:, 128 * (c % 4):128 * (c % 4) + 128].rearrange("p (a b) -> p a b", a=1)[:, 0, :]
                        if False else pk[fb][:, 128 * (c % 4):128 * (c % 4) + 128],
                        xcb[:, 128 * c:128 * c + 128], identf[:]),
                        reads=sres(mem, "xcb") + sres(mem, "identf"), writes=PS(fb))
                    if c % 4 == 3:
                        P.add("vector", lambda e, c=c, fb=fb: e.tensor_copy(
                            xcT[:, 512 * (c // 4):512 * (c // 4) + 512], pk[fb][:, 0:512]),
                            reads=PS(fb), writes=sres(mem, "xcT"))
                for c in range(8):
                    MM(xcT[:, 128 * c:128 * c + 128], centT[:, 8 * c:8 * c + 8],
                       pk[2][:, 0:8], c == 0, c == 7,
                       sres(mem, "xcT") + sres(mem, "centT"), PS(2))
                gA = gat.ap()[:, 8 * t:8 * t + 8]
                gF = gfin.ap()[:, 8 * t:8 * t + 8]
                gAr = sres(mem, "gat", 8 * t, 8 * t + 8)
                gFr = sres(mem, "gfin", 8 * t, 8 * t + 8)
                sm = smax.ap()
                smr = sres(mem, "smax")
                P.add("vector", lambda e: e.tensor_scalar(
                    mv[:, 2:3], mv[:, 3:4], -1.0, None, op0=AluOpType.mult),
                    reads=sres(mem, "mv"), writes=sres(mem, "mv"))
                P.add("scalar", lambda e, gA=gA: e.activation(gA, pk[2][:, 0:8], AF.Exp,
                                                              scale=mv[:, 2:3]),
                      reads=PS(2) + sres(mem, "mv"), writes=gAr)
                P.add("vector", lambda e, gA=gA: e.tensor_scalar(
                    gA, gA, 1.0, None, op0=AluOpType.add), reads=gAr, writes=gAr)
                P.add("vector", lambda e, gA=gA: e.reciprocal(gA, gA), reads=gAr, writes=gAr)
                gsr = sres(mem, "gscr")
                cir = sres(mem, "cidx")
                V = "vector"
                P.add(V, lambda e, gA=gA, sm=sm: e.reduce_max(sm[:, 0:1], gA, axis=AX.X),
                      reads=gAr, writes=smr)
                P.add(V, lambda e, gA=gA, gF=gF, sm=sm: e.tensor_scalar(
                    gF, gA, sm[:, 0:1], None, op0=AluOpType.is_equal), reads=gAr + smr, writes=gFr)
                P.add(V, lambda e, gF=gF: e.scalar_tensor_tensor(
                    gF, gF, -1000.0, cidx[:], op0=AluOpType.mult, op1=AluOpType.add),
                    reads=gFr + cir, writes=gFr)
                P.add(V, lambda e, gF=gF, sm=sm: e.tensor_reduce(
                    sm[:, 1:2], gF, axis=AX.X, op=AluOpType.min), reads=gFr, writes=smr)
                P.add(V, lambda e, gF=gF, sm=sm: e.tensor_scalar(
                    gF, gF, sm[:, 1:2], None, op0=AluOpType.is_equal), reads=gFr + smr, writes=gFr)
                P.add(V, lambda e, gA=gA, gF=gF: e.scalar_tensor_tensor(
                    gscr[:], gF, -1e30, gA, op0=AluOpType.mult, op1=AluOpType.add),
                    reads=gAr + gFr, writes=gsr)
                P.add(V, lambda e, sm=sm: e.reduce_max(sm[:, 2:3], gscr[:], axis=AX.X),
                      reads=gsr, writes=smr)
                P.add(V, lambda e, sm=sm: e.tensor_scalar(
                    gscr[:], gscr[:], sm[:, 2:3], None, op0=AluOpType.is_equal),
                    reads=gsr + smr, writes=gsr)
                P.add(V, lambda e: e.scalar_tensor_tensor(
                    gscr[:], gscr[:], -1000.0, cidx[:], op0=AluOpType.mult, op1=AluOpType.add),
                    reads=gsr + cir, writes=gsr)
                P.add(V, lambda e, sm=sm: e.tensor_reduce(
                    sm[:, 3:4], gscr[:], axis=AX.X, op=AluOpType.min), reads=gsr, writes=smr)
                P.add(V, lambda e, sm=sm: e.tensor_scalar(
                    gscr[:], gscr[:], sm[:, 3:4], None, op0=AluOpType.is_equal),
                    reads=gsr + smr, writes=gsr)
                P.add(V, lambda e, gF=gF: e.tensor_tensor(gF, gF, gscr[:], op=AluOpType.add),
                      reads=gFr + gsr, writes=gFr)
                P.add(V, lambda e, sm=sm: e.tensor_tensor(
                    sm[:, 4:5], sm[:, 0:1], sm[:, 2:3], op=AluOpType.add), reads=smr, writes=smr)
                P.add(V, lambda e, sm=sm: e.reciprocal(sm[:, 5:6], sm[:, 4:5]),
                      reads=smr, writes=smr)
                P.add(V, lambda e, gA=gA, gF=gF, sm=sm: e.scalar_tensor_tensor(
                    gF, gA, sm[:, 5:6], gF, op0=AluOpType.mult, op1=AluOpType.mult),
                    reads=gAr + smr + gFr, writes=gFr)

    def emit_ffn0():
        NKG1 = 6
        for e_ in range(NE):
            tok0 = 256 * e_
            for gi in range(NKG1):
                cw = 512 if gi < 5 else 256
                buf = gi % 2
                for nm, dsrc in (("w1b", din["L0_w1"]), ("w3b", din["L0_w3"])):
                    dma("gpsimd",
                        mem.tensors[nm][0].ap().rearrange("p (b k m) -> p b k m", b=2, k=8)[:, buf, :, 0:cw],
                        dsrc[e_][:, 512 * gi:512 * gi + cw].rearrange("(k p) m -> p k m", p=128),
                        sres(mem, nm, buf * 4096, buf * 4096 + 4096),
                        sres(mem, nm, buf * 4096, buf * 4096 + 4096), tag=f"ffn0w{e_}")
                for mc in range(4 if gi < 5 else 2):
                    mct = 4 * gi + mc
                    bnk = mct % 2
                    bnk2 = 2 + mct % 2
                    for k in range(8):
                        MM(w1b.ap()[:, 4096 * buf + 512 * k + 128 * mc: 4096 * buf + 512 * k + 128 * mc + 128],
                           XnT[:, 2048 * k + tok0:2048 * k + tok0 + 256],
                           pk[bnk][:, 0:256], k == 0, k == 7,
                           sres(mem, "w1b", buf * 4096, buf * 4096 + 4096) +
                           sres(mem, "XnT", 2048 * k + tok0, 2048 * k + tok0 + 256),
                           PS(bnk))
                    for k in range(8):
                        MM(w3b.ap()[:, 4096 * buf + 512 * k + 128 * mc: 4096 * buf + 512 * k + 128 * mc + 128],
                           XnT[:, 2048 * k + tok0:2048 * k + tok0 + 256],
                           pk[bnk2][:, 0:256], k == 0, k == 7,
                           sres(mem, "w3b", buf * 4096, buf * 4096 + 4096) +
                           sres(mem, "XnT", 2048 * k + tok0, 2048 * k + tok0 + 256),
                           PS(bnk2))
                    hsap = hs0.ap()[:, 256 * (mct % 2):256 * (mct % 2) + 256]
                    hsres = sres(mem, "hs0", 256 * (mct % 2), 256 * (mct % 2) + 256)
                    P.add("scalar", lambda e, b=bnk, hsap=hsap: e.activation(hsap, pk[b][:, 0:256], AF.Silu),
                          reads=PS(bnk), writes=hsres)
                    P.add("vector", lambda e, b=bnk2, mct=mct, hsap=hsap: e.tensor_tensor(
                        hT0[:, 256 * mct:256 * mct + 256], hsap, pk[b][:, 0:256], op=AluOpType.mult),
                        reads=hsres + PS(bnk2), writes=sres(mem, "hT0", 256 * mct, 256 * mct + 256))
            for gi in range(NKG1):
                cw = 512 if gi < 5 else 256
                buf = gi % 2
                dma("gpsimd",
                    w2b.ap().rearrange("p (b k m) -> p b k m", b=2, k=4)[:, buf, 0:cw // 128, :],
                    din["L0_w2"][e_][512 * gi:512 * gi + cw, :].rearrange("(k p) m -> p k m", p=128),
                    sres(mem, "w2b", buf * 4096, buf * 4096 + 4096),
                    sres(mem, "w2b", buf * 4096, buf * 4096 + 4096), tag=f"ffn0w2{e_}")
                for j in range(2):
                    tt = 2 * e_ + j
                    pbk = [4 + j, 6 + j]
                    for kc in range(cw // 128):
                        mct = 4 * gi + kc
                        for nh in range(2):
                            MM(hT0[:, 256 * mct + 128 * j:256 * mct + 128 * j + 128],
                               w2b.ap()[:, 4096 * buf + 1024 * kc + 512 * nh:4096 * buf + 1024 * kc + 512 * nh + 512],
                               pk[pbk[nh]][:, 0:512], gi == 0 and kc == 0, gi == NKG1 - 1 and kc == (cw // 128) - 1,
                               sres(mem, "hT0", 256 * mct, 256 * mct + 256) +
                               sres(mem, "w2b", buf * 4096, buf * 4096 + 4096),
                               PS(pbk[nh]))
                for j in range(2):
                    if gi == NKG1 - 1:
                        tt = 2 * e_ + j
                        for nh in range(2):
                            P.add("vector", lambda e, tt=tt, nh=nh, pb_=4 + j + 2 * nh: e.tensor_copy(
                                g[:, 1024 * tt + 512 * nh:1024 * tt + 512 * nh + 512], pk[pb_][:, 0:512]),
                                reads=PS(4 + j + 2 * nh),
                                writes=sres(mem, "g", 1024 * tt + 512 * nh, 1024 * tt + 512 * nh + 512))

    def emit_gates():
        for t in range(NT):
            for k in range(8):
                MM(XnT[:, 2048 * k + 128 * t:2048 * k + 128 * t + 128],
                   centT[:, 8 * k:8 * k + 8], pk[2][:, 0:8], k == 0, k == 7,
                   sres(mem, "XnT", 2048 * k + 128 * t, 2048 * k + 128 * t + 128) +
                   sres(mem, "centT"), PS(2))
            gA = gat.ap()[:, 8 * t:8 * t + 8]
            gF = gfin.ap()[:, 8 * t:8 * t + 8]
            gAr = sres(mem, "gat", 8 * t, 8 * t + 8)
            gFr = sres(mem, "gfin", 8 * t, 8 * t + 8)
            sm = smax.ap()
            smr = sres(mem, "smax")
            P.add("scalar", lambda e, gA=gA: e.activation(gA, pk[2][:, 0:8], AF.Sigmoid),
                  reads=PS(2), writes=gAr)
            P.add("vector", lambda e, gA=gA, sm=sm: e.reduce_max(sm[:, 0:1], gA, axis=AX.X),
                  reads=gAr, writes=smr)
            P.add("vector", lambda e, gA=gA, gF=gF, sm=sm: e.tensor_scalar(
                gF, gA, sm[:, 0:1], None, op0=AluOpType.is_equal), reads=gAr + smr, writes=gFr)
            P.add("vector", lambda e, gA=gA, gF=gF: e.scalar_tensor_tensor(
                gF, gF, -1e30, gA, op0=AluOpType.mult, op1=AluOpType.add), reads=gAr + gFr, writes=gFr)
            P.add("vector", lambda e, gF=gF, sm=sm: e.reduce_max(sm[:, 1:2], gF, axis=AX.X),
                  reads=gFr, writes=smr)
            P.add("vector", lambda e, gA=gA, gF=gF, sm=sm: e.tensor_scalar(
                gF, gA, sm[:, 1:2], None, op0=AluOpType.is_ge), reads=gAr + smr, writes=gFr)
            P.add("vector", lambda e, sm=sm: e.tensor_tensor(
                sm[:, 2:3], sm[:, 0:1], sm[:, 1:2], op=AluOpType.add), reads=smr, writes=smr)
            P.add("vector", lambda e, sm=sm: e.reciprocal(sm[:, 3:4], sm[:, 2:3]),
                  reads=smr, writes=smr)
            P.add("vector", lambda e, gA=gA, gF=gF, sm=sm: e.scalar_tensor_tensor(
                gF, gA, sm[:, 3:4], gF, op0=AluOpType.mult, op1=AluOpType.mult),
                reads=gAr + smr + gFr, writes=gFr)

    def emit_moe():
        for e_ in range(NE):
            for q in range(4):
                dma("gpsimd",
                    W2sb.ap().rearrange("p (k m) -> p k m", k=32)[:, 8 * q:8 * q + 8, :],
                    din["L1_W2"][e_][1024 * q:1024 * q + 1024, :].rearrange("(k p) m -> p k m", p=128),
                    sres(mem, "W2sb", 8192 * q, 8192 * q + 8192),
                    sres(mem, "W2sb", 8192 * q, 8192 * q + 8192), tag=f"moeW2.{e_}")
            b2b = e_ % 2
            dma("gpsimd", b2row[0:1, 1024 * b2b:1024 * b2b + 1024].rearrange("a b -> a b"),
                din["L1_b2"][e_:e_ + 1, :],
                sres(mem, "b2row", 1024 * b2b, 1024 * b2b + 1024),
                sres(mem, "b2row", 1024 * b2b, 1024 * b2b + 1024))
            for grp in range(4):
                tok0 = 512 * grp
                for mg in range(8):
                    buf = mg % 2
                    dma("gpsimd",
                        w1mb.ap().rearrange("p (b k m) -> p b k m", b=2, k=8)[:, buf, :, :],
                        din["L1_W1"][e_][:, 512 * mg:512 * mg + 512].rearrange("(k p) m -> p k m", p=128),
                        sres(mem, "w1mb", buf * 4096, buf * 4096 + 4096),
                        sres(mem, "w1mb", buf * 4096, buf * 4096 + 4096), tag=f"moeW1.{e_}.{grp}")
                    for mc in range(4):
                        mct = 4 * mg + mc
                        bnk = mct % 2
                        for k in range(8):
                            MM(w1mb.ap()[:, 4096 * buf + 512 * k + 128 * mc:4096 * buf + 512 * k + 128 * mc + 128],
                               XnT[:, 2048 * k + tok0:2048 * k + tok0 + 512],
                               pk[bnk][:, 0:512], k == 0, k == 7,
                               sres(mem, "w1mb", buf * 4096, buf * 4096 + 4096) +
                               sres(mem, "XnT", 2048 * k + tok0, 2048 * k + tok0 + 512),
                               PS(bnk))
                        P.add("scalar", lambda e, b=bnk, mct=mct, e_=e_: e.activation(
                            hTm[:, 512 * mct:512 * mct + 512], pk[b][:, 0:512], AF.Gelu,
                            bias=b1sb[:, 32 * e_ + mct:32 * e_ + mct + 1]),
                            reads=PS(bnk) + sres(mem, "b1sb"),
                            writes=sres(mem, "hTm", 512 * mct, 512 * mct + 512))
                for j in range(4):
                    for nh in range(2):
                        MM(onesrow[0:1, 0:128],
                           b2row[0:1, 1024 * b2b + 512 * nh:1024 * b2b + 512 * nh + 512],
                           pk[2 * j + nh][:, 0:512], True, False,
                           sres(mem, "onesrow") + sres(mem, "b2row", 1024 * b2b, 1024 * b2b + 1024),
                           PS(2 * j + nh))
                for kc in range(32):
                    for j in range(4):
                        for nh in range(2):
                            MM(hTm[:, 512 * kc + 128 * j:512 * kc + 128 * j + 128],
                               W2sb.ap()[:, 1024 * kc + 512 * nh:1024 * kc + 512 * nh + 512],
                               pk[2 * j + nh][:, 0:512], False, kc == 31,
                               sres(mem, "hTm", 512 * kc, 512 * kc + 512) +
                               sres(mem, "W2sb", 1024 * kc, 1024 * kc + 1024),
                               PS(2 * j + nh))
                for j in range(4):
                    tt = 4 * grp + j
                    for nh in range(2):
                        gap = g.ap()[:, 1024 * tt + 512 * nh:1024 * tt + 512 * nh + 512]
                        gr = sres(mem, "g", 1024 * tt + 512 * nh, 1024 * tt + 512 * nh + 512)
                        gsc = gfin.ap()[:, 8 * tt + e_:8 * tt + e_ + 1]
                        gscr = sres(mem, "gfin", 8 * tt, 8 * tt + 8)
                        if e_ == 0:
                            P.add("vector", lambda e, pbk=2 * j + nh, gap=gap, gsc=gsc: e.tensor_scalar(
                                gap, pk[pbk][:, 0:512], gsc, None, op0=AluOpType.mult),
                                reads=PS(2 * j + nh) + gscr, writes=gr)
                        else:
                            P.add("vector", lambda e, pbk=2 * j + nh, gap=gap, gsc=gsc: e.scalar_tensor_tensor(
                                gap, pk[pbk][:, 0:512], gsc, gap, op0=AluOpType.mult, op1=AluOpType.add),
                                reads=PS(2 * j + nh) + gscr + gr, writes=gr)

    def emit_attn(L):
        sc_ = 0.125 if L == 0 else 1.0
        QTz = [QTz0, QTz1]
        KTA = [KTA0, KTA1]
        MA = [MA0, MA1]
        r_ = lambda nm, a=None, b=None: mem.res(nm, a, b)

        def attn_init():
            for hh in (0, 1):
                ob = 64 * (1 - hh)
                tts = [(f"QTz{hh}", QTz[hh]), (f"KTA{hh}", KTA[hh])]
                if L == 1:
                    tts.append((f"MA{hh}", MA[hh]))
                for nm, tt in tts:
                    P.add("vector", lambda e, tt=tt, ob=ob: e.memset(
                        tt.ap()[ob:ob + 64, :], 0.0), (), r_(nm))

        def pair_weights(p):
            qoff = 128 * p
            koff = 128 * p if L == 0 else 1024 + 128 * p
            voff = 128 * p if L == 0 else 2048 + 128 * p
            srcs = [("wqb", (din["L0_Wq"] if L == 0 else din["L1_wqkv"]), qoff),
                    ("wkb", (din["L0_Wk"] if L == 0 else din["L1_wqkv"]), koff),
                    ("wvb", (din["L0_Wv"] if L == 0 else din["L1_wqkv"]), voff)]
            for nm, src, off in srcs:
                dma("gpsimd",
                    mem.tensors[nm][0].ap().rearrange("p (k m) -> p k m", k=8),
                    src[:, off:off + 128].rearrange("(k p) m -> p k m", p=128),
                    sres(mem, nm), sres(mem, nm), tag=f"wqkv{L}.{p}")

        def proj_qk(p):
            bq_col = (16 if L else 0) + p
            bk_col = (24 if L else 8) + p
            for which, wbn, bcol, dsts in (("q", "wqb", bq_col, QTz),
                                           ("k", "wkb", bk_col, KTA)):
                wb = mem.tensors[wbn][0]
                for n in range(4):
                    bnk = n % 2
                    for k in range(8):
                        MM(wb.ap()[:, 128 * k:128 * k + 128],
                           XnT[:, 2048 * k + 512 * n:2048 * k + 512 * n + 512],
                           pk[bnk][:, 0:512], k == 0, k == 7,
                           sres(mem, wbn) +
                           sres(mem, "XnT", 2048 * k + 512 * n, 2048 * k + 512 * n + 512),
                           PS(bnk))
                    for hh in (0, 1):
                        dn = (f"QTz{hh}" if which == "q" else f"KTA{hh}")
                        P.add("vector", lambda e, d=dsts[hh], hh=hh, n=n, bnk=bnk, bcol=bcol:
                              e.tensor_scalar(
                                  d.ap()[64 * hh:64 * hh + 64, 512 * n:512 * n + 512],
                                  pk[bnk][64 * hh:64 * hh + 64, 0:512],
                                  bqk[64 * hh:64 * hh + 64, bcol:bcol + 1], None,
                                  op0=AluOpType.add),
                              reads=PS(bnk) + sres(mem, "bqk"),
                              writes=r_(dn, 512 * n, 512 * n + 512))
            if L == 1:
                for hh in (0, 1):
                    P.add("vector", lambda e, hh=hh: e.tensor_scalar(
                        MA[hh].ap()[64 * hh:64 * hh + 64, :],
                        QTz[hh].ap()[64 * hh:64 * hh + 64, :],
                        -2.0, None, op0=AluOpType.mult),
                        reads=r_(f"QTz{hh}"), writes=r_(f"MA{hh}"))

        def proj_v(p):
            for t in range(NT):
                bnk = 2 + t % 2
                for k in range(8):
                    MM(XnT[:, 2048 * k + 128 * t:2048 * k + 128 * t + 128],
                       wvb.ap()[:, 128 * k:128 * k + 128],
                       pk[bnk][:, 0:128], k == 0, k == 7,
                       sres(mem, "wvb") + sres(mem, "XnT", 2048 * k + 128 * t, 2048 * k + 128 * t + 128),
                       PS(bnk))
                P.add("vector", lambda e, t=t, bnk=bnk: e.tensor_copy(
                    Vsb[:, 128 * t:128 * t + 128], pk[bnk][:, 0:128]),
                    reads=PS(bnk), writes=sres(mem, "Vsb", 128 * t, 128 * t + 128))

        def aug_build(p):
            for side, srcs, sel, seed, dsts in (
                    ("q", QTz, selq, seedq, MA), ("k", KTA, selk, seedk, KTA)):
                seln = "selq" if side == "q" else "selk"
                seedn = "seedq" if side == "q" else "seedk"
                for hh in (0, 1):
                    hb = 64 * hh
                    P.add("scalar", lambda e, s=srcs[hh], hb=hb: e.activation(
                        sqsb.ap()[hb:hb + 64, :], s.ap()[hb:hb + 64, :], AF.Square),
                        reads=r_(f"QTz{hh}" if side == "q" else f"KTA{hh}"),
                        writes=r_("sqsb"))
                for hh in (0, 1):
                    ob = 64 * (1 - hh)
                    dn = f"MA{hh}" if side == "q" else f"KTA{hh}"
                    for n in range(4):
                        MM(seed[0:1, :], onesrow[0:1, 0:512], pk[2][ob:ob + 2, 0:512],
                           True, False, sres(mem, seedn) + sres(mem, "onesrow"), PS(2))
                        MM(sel[:, 2 * hh:2 * hh + 2], sqsb.ap()[:, 512 * n:512 * n + 512],
                           pk[2][ob:ob + 2, 0:512], False, True,
                           sres(mem, seln) + r_("sqsb"), PS(2))
                        P.add("vector", lambda e, d=dsts[hh], ob=ob, n=n: e.tensor_copy(
                            d.ap()[ob:ob + 2, 512 * n:512 * n + 512], pk[2][ob:ob + 2, 0:512]),
                            reads=PS(2), writes=r_(dn))

        def chunks_stage(p, t):
            Lrow = 128 * (t + 1)
            par = t % 2
            nq = (Lrow + 511) // 512
            for cq in range(nq):
              for hh in (0, 1):
                sc_t = scor[(hh, par)]
                snm = f"scor{hh}" if par == 0 else f"scorX{hh}"
                c0 = 512 * cq
                cw = min(512, Lrow - c0)
                if L == 0:
                    bnk = 2 * hh + cq % 2
                    MM(QTz[hh].ap()[:, 128 * t:128 * t + 128],
                       KTA[hh].ap()[:, c0:c0 + cw],
                       pk[bnk][:, 0:cw], True, True,
                       r_(f"QTz{hh}", 128 * t, 128 * t + 128) + r_(f"KTA{hh}", c0, c0 + cw),
                       PS(bnk))
                    if c0 + cw == Lrow:
                        P.add("vector", lambda e, bnk=bnk, o=cw - 128: e.tensor_tensor(
                            pk[bnk][:, o:o + 128], pk[bnk][:, o:o + 128], mask0[:], op=AluOpType.add),
                            reads=PS(bnk) + sres(mem, "mask0"), writes=PS(bnk))
                    P.add("scalar", lambda e, bnk=bnk, sc_t=sc_t, c0=c0, cw=cw: e.copy(
                        sc_t.ap()[:, c0:c0 + cw], pk[bnk][:, 0:cw]),
                        reads=PS(bnk), writes=sres(mem, snm, c0, c0 + cw))
                else:
                    bd = 2 * hh
                    bn = 2 * hh + 1
                    MM(QTz[hh].ap()[:, 128 * t:128 * t + 128], KTA[hh].ap()[:, c0:c0 + cw],
                       pk[bd][:, 0:cw], True, True,
                       r_(f"QTz{hh}", 128 * t, 128 * t + 128) + r_(f"KTA{hh}", c0, c0 + cw),
                       PS(bd))
                    MM(MA[hh].ap()[:, 128 * t:128 * t + 128], KTA[hh].ap()[:, c0:c0 + cw],
                       pk[bn][:, 0:cw], True, True,
                       r_(f"MA{hh}", 128 * t, 128 * t + 128) + r_(f"KTA{hh}", c0, c0 + cw),
                       PS(bn))
                    P.add("scalar", lambda e, bd=bd, sc_t=sc_t, c0=c0, cw=cw: e.activation(
                        sc_t.ap()[:, c0:c0 + cw], pk[bd][:, 0:cw], AF.Square),
                        reads=PS(bd), writes=sres(mem, snm, c0, c0 + cw))
                    P.add("scalar", lambda e, bn=bn, hh=hh, cw=cw: e.activation(
                        rdb[hh].ap()[:, 0:cw], pk[bn][:, 0:cw], AF.Ln),
                        reads=PS(bn), writes=sres(mem, f"rd{hh}"))
                    P.add("scalar", lambda e, hh=hh, cw=cw: e.activation(
                        rdb[hh].ap()[:, 0:cw], rdb[hh].ap()[:, 0:cw], AF.Exp, scale=-1.0),
                        reads=sres(mem, f"rd{hh}"), writes=sres(mem, f"rd{hh}"))
                    P.add("vector", lambda e, hh=hh, sc_t=sc_t, c0=c0, cw=cw: e.tensor_tensor(
                        sc_t.ap()[:, c0:c0 + cw], sc_t.ap()[:, c0:c0 + cw],
                        rdb[hh].ap()[:, 0:cw], op=AluOpType.mult),
                        reads=sres(mem, snm, c0, c0 + cw) + sres(mem, f"rd{hh}"),
                        writes=sres(mem, snm, c0, c0 + cw))
                    if c0 + cw == Lrow:
                        P.add("vector", lambda e, sc_t=sc_t, o=Lrow - 128: e.tensor_tensor(
                            sc_t.ap()[:, o:o + 128], sc_t.ap()[:, o:o + 128],
                            mask1[:], op=AluOpType.add),
                            reads=sres(mem, snm, Lrow - 128, Lrow) + sres(mem, "mask1"),
                            writes=sres(mem, snm, Lrow - 128, Lrow))

        def tails_stage(p, t):
            Lrow = 128 * (t + 1)
            par = t % 2
            for hh in (0, 1):
                sc_t = scor[(hh, par)]
                snm = f"scor{hh}" if par == 0 else f"scorX{hh}"
                Eb = Ebuf[(hh, par)]
                Enm = f"E{hh}{par}"
                sb_ = 8 * (2 * par + hh)
                P.add("vector", lambda e, sc_t=sc_t, Lr=Lrow, sb_=sb_: e.reduce_max(
                    smax[:, sb_:sb_ + 1], sc_t.ap()[:, 0:Lr], axis=AX.X, negate=True),
                    reads=sres(mem, snm, 0, Lrow), writes=sres(mem, "smax"))
                P.add("vector", lambda e, sb_=sb_: e.tensor_scalar(
                    smax[:, sb_ + 1:sb_ + 2], smax[:, sb_:sb_ + 1], sc_, None, op0=AluOpType.mult),
                    reads=sres(mem, "smax"), writes=sres(mem, "smax"))
                P.add("scalar", lambda e, sc_t=sc_t, Eb=Eb, Lr=Lrow, sb_=sb_, Enm=Enm: e.activation(
                    Eb.ap()[:, 0:Lr], sc_t.ap()[:, 0:Lr], AF.Exp,
                    bias=smax[:, sb_ + 1:sb_ + 2], scale=sc_,
                    accum_out=smax[:, sb_ + 2:sb_ + 3]),
                    reads=sres(mem, snm, 0, Lrow) + sres(mem, "smax"),
                    writes=sres(mem, Enm, 0, Lrow) + sres(mem, "smax"))
                P.add("vector", lambda e, sb_=sb_: e.reciprocal(
                    smax[:, sb_ + 3:sb_ + 4], smax[:, sb_ + 2:sb_ + 3]),
                    reads=sres(mem, "smax"), writes=sres(mem, "smax"))
                P.add("vector", lambda e, Eb=Eb, Enm=Enm, Lr=Lrow, sb_=sb_: e.tensor_scalar(
                    Eb.ap()[:, 0:Lr], Eb.ap()[:, 0:Lr], smax[:, sb_ + 3:sb_ + 4],
                    None, op0=AluOpType.mult),
                    reads=sres(mem, Enm, 0, Lrow) + sres(mem, "smax"),
                    writes=sres(mem, Enm, 0, Lrow))

        def txav_stage(p, t, hh):
            par = t % 2
            Eb = Ebuf[(hh, par)]
            Enm = f"E{hh}{par}"
            ntile = t + 1
            for grp in range((ntile + 7) // 8):
                bnk = 6 + grp % 2
                ntg = min(8, ntile - 8 * grp)
                for j in range(ntg):
                    kt = 8 * grp + j
                    P.add("tensor", lambda e, Eb=Eb, kt=kt, j=j, bnk=bnk: e.transpose(
                        pb[bnk][:, 128 * j:128 * j + 128],
                        Eb.ap()[:, 128 * kt:128 * kt + 128], identb[:]),
                        reads=sres(mem, Enm, 128 * kt, 128 * kt + 128) + sres(mem, "identb"),
                        writes=PS(bnk))
                P.add("vector", lambda e, hh=hh, grp=grp, bnk=bnk, ntg=ntg: e.tensor_copy(
                    ETb[hh].ap()[:, 1024 * grp:1024 * grp + 128 * ntg], pb[bnk][:, 0:128 * ntg]),
                    reads=PS(bnk),
                    writes=sres(mem, f"ET{hh}", 1024 * grp, 1024 * grp + 128 * ntg))
            avb = 4 + hh
            for kt in range(ntile):
                MM(Vsb[:, 128 * kt + 64 * hh:128 * kt + 64 * hh + 64],
                   ETb[hh].ap()[:, 128 * kt:128 * kt + 128],
                   pk[avb][0:64, 0:128], kt == 0, kt == ntile - 1,
                   sres(mem, "Vsb", 128 * kt, 128 * kt + 128) +
                   sres(mem, f"ET{hh}", 128 * kt, 128 * kt + 128),
                   PS(avb))
            P.add("vector", lambda e, hh=hh, p=p, t=t, avb=avb: e.tensor_copy(
                OTall[64 * hh:64 * hh + 64, 2048 * p + 128 * t:2048 * p + 128 * t + 128],
                pk[avb][0:64, 0:128]),
                reads=PS(avb),
                writes=sres(mem, "OTall", 2048 * p + 128 * t, 2048 * p + 128 * t + 128))

        attn_init()
        pair_weights(0)
        proj_qk(0)
        proj_v(0)
        if L == 1:
            aug_build(0)
        for p in range(NPAIR):
            for t in range(NT):
                if t >= 2:
                    for hh in (0, 1):
                        txav_stage(p, t - 2, hh)
                chunks_stage(p, t)
                if t >= 1:
                    tails_stage(p, t - 1)
                if t == 2 and p + 1 < NPAIR:
                    pair_weights(p + 1)
            tails_stage(p, NT - 1)
            if p + 1 < NPAIR:
                proj_qk(p + 1)
            for t in (NT - 2, NT - 1):
                for hh in (0, 1):
                    txav_stage(p, t, hh)
            if p + 1 < NPAIR:
                proj_v(p + 1)
                if L == 1:
                    aug_build(p + 1)

        wo_src = din["L0_Wo"] if L == 0 else din["L1_wout"]
        for q in range(2):
            dma("gpsimd",
                Wosb.ap().rearrange("p (k m) -> p k m", k=8)[:, 4 * q:4 * q + 4, :],
                wo_src[512 * q:512 * q + 512, :].rearrange("(k p) m -> p k m", p=128),
                sres(mem, "Wosb", 4096 * q, 4096 * q + 4096),
                sres(mem, "Wosb", 4096 * q, 4096 * q + 4096), tag=f"wo{L}")
        bv_src = din["L0_bv"] if L == 0 else din["L1_bqkv"][2048:3072]
        bo_src = din["L0_bo"] if L == 0 else din["L1_bout"]
        dma("sync", btmp[0:8, :], bv_src.rearrange("(a b) -> a b", b=128),
            sres(mem, "btmp"), sres(mem, "btmp"))
        P.add("tensor", lambda e: e.transpose(pk[7][0:128, 0:8], btmp[0:8, :], identf[0:8, 0:8]),
              reads=sres(mem, "btmp") + sres(mem, "identf"), writes=PS(7))
        P.add("vector", lambda e: e.tensor_copy(bvc[:], pk[7][:, 0:8]),
              reads=PS(7), writes=sres(mem, "bvc"))
        dma("gpsimd", borow[0:1, :], bo_src.rearrange("(a b) -> a b", a=1),
            sres(mem, "borow"), sres(mem, "borow"))
        for nh in range(2):
            MM(onesrow[0:1, 0:1], borow[0:1, 512 * nh:512 * nh + 512], pk[6][0:1, 0:512] if nh == 0 else pk[7][0:1, 0:512],
               True, False, sres(mem, "onesrow") + sres(mem, "borow"), PS(6 + nh))
            for k in range(8):
                MM(bvc[:, k:k + 1], Wosb.ap()[:, 1024 * k + 512 * nh:1024 * k + 512 * nh + 512],
                   pk[6 + nh][0:1, 0:512], False, k == 7,
                   sres(mem, "bvc") + sres(mem, "Wosb"), PS(6 + nh))
            P.add("vector", lambda e, nh=nh: e.tensor_copy(
                bprow[0:1, 512 * nh:512 * nh + 512], (pk[6 + nh])[0:1, 0:512]),
                reads=PS(6 + nh), writes=sres(mem, "bprow"))
        for nh in range(2):
            MM(onesrow[0:1, 0:128], bprow[0:1, 512 * nh:512 * nh + 512], pk[6 + nh][:, 0:512],
               True, True, sres(mem, "onesrow") + sres(mem, "bprow"), PS(6 + nh))
            P.add("vector", lambda e, nh=nh: e.tensor_copy(
                bobc[:, 512 * nh:512 * nh + 512], pk[6 + nh][:, 0:512]),
                reads=PS(6 + nh), writes=sres(mem, "bobc", 512 * nh, 512 * nh + 512))

        xsrc = X0 if L == 0 else dx1
        xdst = dx1 if L == 0 else out
        xres_r = (lambda t: (f"dx0c.{t}",)) if L == 0 else (lambda t: (f"dx1.{t}",))
        xres_w = (lambda t: (f"dx1.{t}",)) if L == 0 else (lambda t: (f"dout.{t}",))
        for t in range(NT):
            xb = xbuf.ap()[:, (t % 2) * D:(t % 2) * D + D]
            xrs = sres(mem, "xbuf", (t % 2) * D, (t % 2) * D + D)
            dma("sync", xb, xsrc[128 * t:128 * t + 128, :], xres_r(t), xrs, tag=f"cmbx{L}.{t}")
            for nh in range(2):
                bnk = nh
                for k in range(8):
                    MM(OTall[:, 2048 * k + 128 * t:2048 * k + 128 * t + 128],
                       Wosb.ap()[:, 1024 * k + 512 * nh:1024 * k + 512 * nh + 512],
                       pk[bnk][:, 0:512], k == 0, k == 7,
                       sres(mem, "OTall", 2048 * k + 128 * t, 2048 * k + 128 * t + 128) +
                       sres(mem, "Wosb"),
                       PS(bnk))
            P.add("scalar", lambda e, t=t: e.activation(gs[:], g[:, 1024 * t:1024 * t + 1024], AF.Silu),
                  reads=sres(mem, "g", 1024 * t, 1024 * t + 1024), writes=sres(mem, "gs"))
            for nh in range(2):
                P.add("vector", lambda e, nh=nh: e.tensor_tensor(
                    t1[:, 512 * nh:512 * nh + 512], pk[nh][:, 0:512],
                    bobc[:, 512 * nh:512 * nh + 512], op=AluOpType.add),
                    reads=PS(nh) + sres(mem, "bobc", 512 * nh, 512 * nh + 512),
                    writes=sres(mem, "t1", 512 * nh, 512 * nh + 512))
            P.add("vector", lambda e: e.tensor_tensor(t1[:], t1[:], gs[:], op=AluOpType.mult),
                  reads=sres(mem, "t1") + sres(mem, "gs"), writes=sres(mem, "t1"))
            P.add("vector", lambda e, xb=xb: e.tensor_tensor(xb, t1[:], xb, op=AluOpType.add),
                  reads=sres(mem, "t1") + xrs, writes=xrs)
            dma("sync", xdst[128 * t:128 * t + 128, :], xb, xrs, xres_w(t), tag=f"cmbo{L}.{t}")

    if 0 not in layers:
        for t in range(NT):
            xb = xbuf.ap()[:, (t % 2) * D:(t % 2) * D + D]
            xrs = sres(mem, "xbuf", (t % 2) * D, (t % 2) * D + D)
            dma("sync", xb, X0[128 * t:128 * t + 128, :], (f"dx0.{t}",), xrs)
            dma("sync", dx1[128 * t:128 * t + 128, :], xb, xrs, (f"dx1.{t}",))
    if 0 in layers:
        emit_ln(0)
        emit_ffn0()
        emit_attn(0)
    if 1 in layers:
        emit_ln(1)
        emit_moe()
        emit_attn(1)
    if 1 not in layers:
        for t in range(NT):
            xb = xbuf.ap()[:, (t % 2) * D:(t % 2) * D + D]
            xrs = sres(mem, "xbuf", (t % 2) * D, (t % 2) * D + D)
            dma("sync", xb, dx1[128 * t:128 * t + 128, :], (f"dx1.{t}",), xrs)
            dma("sync", out[128 * t:128 * t + 128, :], xb, xrs, (f"dout.{t}",))

    for tname, bufname, cols, dt in taps:
        tsr = sres(mem, bufname, 0, cols)
        tb, _, _ = mem.tensors[bufname]
        npart = tb.shape[0]
        tout = nc.dram_tensor("tap_" + tname, [npart, cols], dt, kind="ExternalOutput").ap()
        dma("sync", tout, tb.ap()[:, 0:cols], tsr, (f"tap.{tname}",))

    P.finalize(nc, block, sems)
    stack.close()
    return nc


def make_consts():
    tri0 = np.triu(np.ones((128, 128), np.float32), k=1)
    selq = np.zeros((128, 4), np.float32)
    selq[0:64, 1] = 1.0
    selq[64:128, 3] = 1.0
    selk = np.zeros((128, 4), np.float32)
    selk[0:64, 0] = 1.0
    selk[64:128, 2] = 1.0
    return {
        "c_ident_bf": np.eye(128).astype(BF16NP),
        "c_ident_f32": np.eye(128).astype(np.float32),
        "c_mask0": (tri0 * -1e30).astype(np.float32),
        "c_mask1": (tri0 * -1e9).astype(np.float32),
        "c_ones_row": np.ones((1, 512), BF16NP),
        "c_seedq": np.array([[1.0, 1e-6]], BF16NP),
        "c_seedk": np.array([[0.0, 1.0]], BF16NP),
        "c_selq": selq.astype(BF16NP),
        "c_selk": selk.astype(BF16NP),
        "c_idx": (1000.0 + np.arange(8, dtype=np.float32))[None, :].repeat(128, 0),
    }


def make_in_maps(inputs, layers=(0, 1)):
    consts = make_consts()
    base = dict(consts)
    f32c = lambda a: np.ascontiguousarray(np.asarray(a), dtype=np.float32)
    if 0 in layers:
        base.update({
            "L0_Wq": f32c(inputs["L0_attn_Wq"]), "L0_Wk": f32c(inputs["L0_attn_Wk"]),
            "L0_Wv": f32c(inputs["L0_attn_Wv"]), "L0_Wo": f32c(inputs["L0_attn_Wo"]),
            "L0_bq": f32c(inputs["L0_attn_bq"]), "L0_bk": f32c(inputs["L0_attn_bk"]),
            "L0_bv": f32c(inputs["L0_attn_bv"]), "L0_bo": f32c(inputs["L0_attn_bo"]),
            "L0_w1": f32c(inputs["L0_ffn_w1"]), "L0_w3": f32c(inputs["L0_ffn_w3"]),
            "L0_w2": f32c(inputs["L0_ffn_w2"]),
        })
    if 1 in layers:
        base.update({
            "L1_wqkv": f32c(inputs["L1_attn_w_qkv"]), "L1_bqkv": f32c(inputs["L1_attn_b_qkv"]),
            "L1_wout": f32c(inputs["L1_attn_w_out"]), "L1_bout": f32c(inputs["L1_attn_b_out"]),
            "L1_cent": f32c(inputs["L1_ffn_centroids"]), "L1_W1": f32c(inputs["L1_ffn_W1"]),
            "L1_b1": f32c(inputs["L1_ffn_b1"]), "L1_W2": f32c(inputs["L1_ffn_W2"]),
            "L1_b2": f32c(inputs["L1_ffn_b2"]),
        })
    X = f32c(inputs["L0_attn_X"])
    maps = []
    for c in range(8):
        m = dict(base)
        m["x0"] = np.ascontiguousarray(X[c])
        maps.append(m)
    return maps


def kernel(**inputs):
    nc = build_nc(layers=(0, 1))
    maps = make_in_maps(inputs, layers=(0, 1))
    res = run_bass_kernel_spmd(nc, maps, core_ids=list(range(8)))
    return np.stack([np.asarray(r["out"]) for r in res.results], axis=0)


# version 09 (final)
# speedup vs baseline: 1.1141x; 1.0051x over previous
import numpy as np
import ml_dtypes
import concourse.bass as bass
import concourse.mybir as mybir
from concourse.bass_utils import run_bass_kernel_spmd
from concourse.alu_op_type import AluOpType

F32 = mybir.dt.float32
BF16 = mybir.dt.bfloat16
AF = mybir.ActivationFunctionType
AX = mybir.AxisListType
BF16NP = ml_dtypes.bfloat16

S = 2048
D = 1024
NT = 16
H = 16
HD = 64
NPAIR = 8
DFF0 = 2816
NE = 8
DFF1 = 4096

ENGS = ("sync", "gpsimd", "tensor", "vector", "scalar")
DMA_ENGS = ("sync", "gpsimd")
KQ = 8


class Prog:

    def __init__(self):
        self.eops = {e: [] for e in ENGS}
        self.lastw = {}
        self.readers = {}

    def add(self, eng, emit, reads=(), writes=(), tag=""):
        idx = len(self.eops[eng])
        deps = {}

        def need(e2, i2):
            if e2 == eng:
                if eng == "tensor":
                    return
            if deps.get(e2, -1) < i2:
                deps[e2] = i2

        for r in reads:
            lw = self.lastw.get(r)
            if lw is not None:
                need(*lw)
        for w in writes:
            lw = self.lastw.get(w)
            if lw is not None:
                need(*lw)
            for e2, i2 in self.readers.get(w, {}).items():
                need(e2, i2)
        self.eops[eng].append([emit, deps, eng in DMA_ENGS, tag])
        for r in reads:
            self.readers.setdefault(r, {})
            if self.readers[r].get(eng, -1) < idx:
                self.readers[r][eng] = idx
        for w in writes:
            self.lastw[w] = (eng, idx)
            self.readers[w] = {eng: idx}

    def finalize(self, nc, block, sems):
        marked = {e: set() for e in ENGS}
        for e in ENGS:
            for op in self.eops[e]:
                for e2, i2 in op[1].items():
                    marked[e2].add(i2)
        semval = {}
        for e in ENGS:
            if e in DMA_ENGS:
                semval[e] = [16 * (i // KQ + 1) for i in range(len(self.eops[e]))]
            else:
                c = 0
                vals = []
                for i in range(len(self.eops[e])):
                    if i in marked[e]:
                        c += 1
                    vals.append(c)
                semval[e] = vals
        totals = {e: (semval[e][-1] if self.eops[e] else 0) for e in ENGS}

        def semof(e, i):
            return sems[e][i % KQ] if e in DMA_ENGS else sems[e]

        def body(ename):
            ops = self.eops[ename]
            mk = marked[ename]

            def f(eng):
                water = {}

                def wge(sem, v):
                    if water.get(sem, 0) < v:
                        eng.wait_ge(sem, v)
                        water[sem] = v

                for i, (emit, deps, is_dma, tag) in enumerate(ops):
                    for e2, i2 in sorted(deps.items()):
                        wge(semof(e2, i2), semval[e2][i2])
                    if is_dma and i >= KQ:
                        wge(sems[ename][i % KQ], 16 * (i // KQ))
                    inst = emit(eng)
                    if is_dma:
                        inst.then_inc(sems[ename][i % KQ], 16)
                    elif i in mk:
                        inst.then_inc(sems[ename], 1)
                if ename == "sync":
                    for e2 in ENGS:
                        n2 = len(self.eops[e2])
                        if n2 == 0:
                            continue
                        if e2 in DMA_ENGS:
                            for k in range(min(KQ, n2)):
                                last = n2 - 1 - (n2 - 1 - k) % KQ
                                wge(sems[e2][k], semval[e2][last])
                        elif totals[e2] > 0:
                            wge(sems[e2], totals[e2])
            return f

        for e in ENGS:
            getattr(block, e)(body(e))


BLK = 2048


SBUF_BASE = 16544
SBUF_LIMIT = 229376


class Mem:
    def __init__(self, nc):
        self.nc = nc
        self.off = SBUF_BASE
        self.tensors = {}

    def at(self, name, shape, dtype, offset=None):
        nbytes = shape[-1] * (2 if dtype == BF16 else 4)
        if len(shape) > 2:
            for s in shape[1:-1]:
                nbytes *= s
        if offset is None:
            offset = self.off
            self.off = offset + ((nbytes + 31) // 32 * 32)
        t = self.nc.alloc_sbuf_tensor_at(name, shape, dtype, offset=offset)
        self.tensors[name] = (t, offset, nbytes)
        return t

    def res(self, name, col0=None, col1=None):
        t, off, nbytes = self.tensors[name]
        esz = 2 if t.dtype == BF16 else 4
        b0 = off if col0 is None else off + col0 * esz
        b1 = off + nbytes if col1 is None else off + col1 * esz
        return tuple(f"sb{b}" for b in range(b0 // BLK, (b1 - 1) // BLK + 1))


def sres(mem, name, c0=None, c1=None):
    return mem.res(name, c0, c1)


def build_nc(layers=(0, 1), taps=None):
    nc = bass.Bass("TRN2", target_bir_lowering=False)
    P = Prog()

    din = {}

    def dram_in(name, shape, dt=F32):
        din[name] = nc.dram_tensor(name, shape, dt, kind="ExternalInput").ap()
        return din[name]

    X0 = dram_in("x0", [S, D])
    out = nc.dram_tensor("out", [S, D], F32, kind="ExternalOutput").ap()
    dx1 = nc.dram_tensor("dx1", [S, D], F32, kind="Internal").ap()

    if 0 in layers:
        for nm, sh in [("Wq", [D, D]), ("Wk", [D, D]), ("Wv", [D, D]), ("Wo", [D, D]),
                       ("bq", [D]), ("bk", [D]), ("bv", [D]), ("bo", [D]),
                       ("w1", [NE, D, DFF0]), ("w3", [NE, D, DFF0]), ("w2", [NE, DFF0, D])]:
            dram_in("L0_" + nm, sh)
    if 1 in layers:
        for nm, sh in [("wqkv", [D, 3 * D]), ("bqkv", [3 * D]), ("wout", [D, D]), ("bout", [D]),
                       ("cent", [NE, D]), ("W1", [NE, D, DFF1]), ("b1", [NE, DFF1]),
                       ("W2", [NE, DFF1, D]), ("b2", [NE, D])]:
            dram_in("L1_" + nm, sh)
    dram_in("c_ident_bf", [128, 128], BF16)
    dram_in("c_ident_f32", [128, 128], F32)
    dram_in("c_mask0", [128, 128], F32)
    dram_in("c_mask1", [128, 128], F32)
    dram_in("c_ones_row", [1, 512], BF16)
    dram_in("c_seedq", [1, 2], BF16)
    dram_in("c_seedk", [1, 2], BF16)
    dram_in("c_selq", [128, 4], BF16)
    dram_in("c_selk", [128, 4], BF16)
    dram_in("c_idx", [128, 8], F32)

    taps = taps or []

    from contextlib import ExitStack
    stack = ExitStack()
    mem = Mem(nc)

    identb = mem.at("identb", [128, 128], BF16)
    identf = mem.at("identf", [128, 128], F32)
    mask0 = mem.at("mask0", [128, 128], F32)
    mask1 = mem.at("mask1", [128, 128], F32)
    onesrow = mem.at("onesrow", [1, 512], BF16)
    seedq = mem.at("seedq", [1, 2], BF16)
    seedk = mem.at("seedk", [1, 2], BF16)
    selq = mem.at("selq", [128, 4], BF16)
    selk = mem.at("selk", [128, 4], BF16)
    ceps = mem.at("ceps", [128, 1], F32)
    cidx = mem.at("cidx", [128, 8], F32)
    gscr = mem.at("gscr", [128, 8], F32)
    bqk = mem.at("bqk", [128, 32], F32)
    b1sb = mem.at("b1sb", [128, 256], F32)
    btmp = mem.at("btmp", [32, 128], F32)
    bvc = mem.at("bvc", [128, 8], BF16)
    borow = mem.at("borow", [1, 1024], BF16)
    bprow = mem.at("bprow", [1, 1024], BF16)
    centb = mem.at("centb", [8, 1024], F32)
    centT = mem.at("centT", [128, 64], F32)
    mv = mem.at("mv", [128, 4], F32)
    bnst = mem.at("bnst", [128, 12], F32)
    smax = mem.at("smax", [128, 64], F32)
    gat = mem.at("gat", [128, 128], F32)
    gfin = mem.at("gfin", [128, 128], F32)
    xnb = mem.at("xnb", [128, 1024], BF16)
    mem.off = (mem.off + BLK - 1) // BLK * BLK
    XnT = mem.at("XnT", [128, 8 * S], BF16)
    g = mem.at("g", [128, NT * D], BF16)
    xbuf = mem.at("xbuf", [128, 2 * D], F32)
    arena0 = mem.off
    mem.off = arena0
    w1b = mem.at("w1b", [128, 2 * 4096], BF16)
    w3b = mem.at("w3b", [128, 2 * 4096], BF16)
    w2b = mem.at("w2b", [128, 2 * 4096], BF16)
    hT0 = mem.at("hT0", [128, 22 * 256], BF16)
    hs0 = mem.at("hs0", [128, 2 * 256], BF16)
    mem.off = arena0
    W2sb = mem.at("W2sb", [128, 32 * 1024], BF16)
    xcb = mem.at("xcb", [128, 1024], F32, offset=mem.tensors["W2sb"][1])
    xcT = mem.at("xcT", [128, 1024], F32, offset=mem.tensors["W2sb"][1] + 4096)
    hTm = mem.at("hTm", [128, 32 * 512], BF16)
    w1mb = mem.at("w1mb", [128, 2 * 4096], BF16)
    b2row = mem.at("b2row", [1, 2048], BF16)
    mem.off = arena0
    QTz0 = mem.at("QTz0", [128, S], BF16)
    QTz1 = mem.at("QTz1", [128, S], BF16)
    KTA0 = mem.at("KTA0", [128, S], BF16)
    KTA1 = mem.at("KTA1", [128, S], BF16)
    MA0 = mem.at("MA0", [128, S], BF16)
    MA1 = mem.at("MA1", [128, S], BF16)
    Vsb = mem.at("Vsb", [128, S], BF16)
    Wosb = mem.at("Wosb", [128, 8 * 1024], BF16, offset=mem.tensors["QTz0"][1])
    wqb = mem.at("wqb", [128, 1024], BF16)
    wkb = mem.at("wkb", [128, 1024], BF16)
    wvb = mem.at("wvb", [128, 1024], BF16)
    scor = {}
    Ebuf = {}
    ETb = {}
    sqd = {}
    rdb = {}
    for h in (0, 1):
        scor[(h, 0)] = mem.at(f"scor{h}", [128, S], F32)
        for par in (0, 1):
            Ebuf[(h, par)] = mem.at(f"E{h}{par}", [128, S], BF16)
        ETb[h] = mem.at(f"ET{h}", [128, S], BF16)
    OTall = mem.at("OTall", [128, 8 * S], BF16)
    scor[(0, 1)] = mem.at("scorX0", [128, S], F32)
    scor[(1, 1)] = mem.at("scorX1", [128, S], F32, offset=mem.tensors["xbuf"][1])
    gs = mem.at("gs", [128, 1024], F32, offset=mem.tensors["E00"][1])
    t1 = mem.at("t1", [128, 1024], F32, offset=mem.tensors["E01"][1])
    bobc = mem.at("bobc", [128, D], F32, offset=mem.tensors["E10"][1])
    rdb[0] = mem.at("rd0", [128, 512], F32, offset=mem.tensors["centb"][1])
    rdb[1] = mem.at("rd1", [128, 512], F32, offset=mem.tensors["centb"][1] + 2048)
    sqsb = mem.at("sqsb", [128, S], BF16, offset=mem.tensors["scor0"][1])
    assert mem.off <= SBUF_LIMIT, f"SBUF overflow: {mem.off}"

    pk = [stack.enter_context(nc.psum_tensor(f"pk{i}", [128, 512], F32)) for i in range(8)]
    pb = [t.bitcast(BF16) for t in pk]

    sems = {}
    for e in ENGS:
        if e in DMA_ENGS:
            sems[e] = [stack.enter_context(nc.semaphore(f"sem_{e}{k}"))
                       for k in range(KQ)]
        else:
            sems[e] = stack.enter_context(nc.semaphore("sem_" + e))
    block = stack.enter_context(nc.Block())

    PS = lambda i: (f"ps{i}",)

    def dma(queue, outap, inap, reads, writes, tag=""):
        P.add(queue, lambda e, o=outap, i=inap: e.dma_start(out=o, in_=i),
              reads=reads, writes=writes, tag=tag)

    def MM(lhsT, rhs, outp, start, stop, reads, writes, tag=""):
        P.add("tensor",
              lambda e, a=lhsT, b=rhs, o=outp, s1=start, s2=stop:
                  e.matmul(o, a, b, start=s1, stop=s2),
              reads=reads, writes=writes, tag=tag)

    dma("sync", identb[:], din["c_ident_bf"], (), sres(mem, "identb"))
    dma("sync", identf[:], din["c_ident_f32"], (), sres(mem, "identf"))
    dma("sync", mask0[:], din["c_mask0"], (), sres(mem, "mask0"))
    dma("sync", mask1[:], din["c_mask1"], (), sres(mem, "mask1"))
    dma("sync", onesrow[:], din["c_ones_row"], (), sres(mem, "onesrow"))
    dma("sync", seedq[:], din["c_seedq"], (), sres(mem, "seedq"))
    dma("sync", seedk[:], din["c_seedk"], (), sres(mem, "seedk"))
    dma("sync", selq[:], din["c_selq"], (), sres(mem, "selq"))
    dma("sync", selk[:], din["c_selk"], (), sres(mem, "selk"))
    P.add("vector", lambda e: e.memset(ceps[:], 1e-5), (), sres(mem, "ceps"))
    dma("sync", cidx[:], din["c_idx"], (), sres(mem, "cidx"))

    def load_bias_cols(dsrc, dstap, dst_res, n):
        dma("sync", btmp[0:n, :], dsrc.rearrange("(a b) -> a b", b=128),
            sres(mem, "btmp"), sres(mem, "btmp"))
        P.add("tensor", lambda e, nn=n: e.transpose(pk[7][0:128, 0:nn], btmp[0:nn, :], identf[0:nn, 0:nn]),
              reads=sres(mem, "btmp") + sres(mem, "identf"), writes=PS(7))
        P.add("vector", lambda e, d=dstap, nn=n: e.tensor_copy(d, pk[7][:, 0:nn]),
              reads=PS(7), writes=dst_res)

    if 0 in layers:
        load_bias_cols(din["L0_bq"], bqk[:, 0:8], sres(mem, "bqk"), 8)
        load_bias_cols(din["L0_bk"], bqk[:, 8:16], sres(mem, "bqk"), 8)
    if 1 in layers:
        load_bias_cols(din["L1_bqkv"][0:1024], bqk[:, 16:24], sres(mem, "bqk"), 8)
        load_bias_cols(din["L1_bqkv"][1024:2048], bqk[:, 24:32], sres(mem, "bqk"), 8)
        for e_ in range(NE):
            load_bias_cols(din["L1_b1"][e_], b1sb[:, 32 * e_:32 * e_ + 32],
                           sres(mem, "b1sb"), 32)
        dma("sync", centb[:], din["L1_cent"], (), sres(mem, "centb"))
        for k in range(8):
            P.add("tensor", lambda e, k=k: e.transpose(pk[7][0:128, 0:8],
                                                       centb[:, 128 * k:128 * k + 128],
                                                       identf[0:8, 0:8]),
                  reads=sres(mem, "centb") + sres(mem, "identf"), writes=PS(7))
            P.add("vector", lambda e, k=k: e.tensor_copy(centT[:, 8 * k:8 * k + 8], pk[7][:, 0:8]),
                  reads=PS(7), writes=sres(mem, "centT"))

    def emit_ln(L):
        xsrc = X0 if L == 0 else dx1
        xres = (lambda t: (f"dx0.{t}",)) if L == 0 else (lambda t: (f"dx1.{t}",))
        for t in range(NT):
            xb = xbuf.ap()[:, (t % 2) * D:(t % 2) * D + D]
            xrs = sres(mem, "xbuf", (t % 2) * D, (t % 2) * D + D)
            dma("sync", xb, xsrc[128 * t:128 * t + 128, :], xres(t), xrs, tag=f"lnx{t}")
            P.add("vector", lambda e, xb=xb: e.bn_stats(bnst[:, 0:6], xb[:, 0:512]),
                  reads=xrs, writes=sres(mem, "bnst"))
            P.add("vector", lambda e, xb=xb: e.bn_stats(bnst[:, 6:12], xb[:, 512:1024]),
                  reads=xrs, writes=sres(mem, "bnst"))
            P.add("vector", lambda e: e.bn_aggr(mv[:, 0:2], bnst[:]),
                  reads=sres(mem, "bnst"), writes=sres(mem, "mv"))
            P.add("scalar", lambda e: e.activation(mv[:, 2:3], mv[:, 1:2], AF.Sqrt, bias=ceps[:]),
                  reads=sres(mem, "mv") + sres(mem, "ceps"), writes=sres(mem, "mv"))
            P.add("vector", lambda e: e.reciprocal(mv[:, 3:4], mv[:, 2:3]),
                  reads=sres(mem, "mv"), writes=sres(mem, "mv"))
            if L == 0:
                P.add("vector", lambda e, xb=xb: e.tensor_scalar(
                    xnb[:], xb, mv[:, 0:1], mv[:, 3:4], op0=AluOpType.subtract, op1=AluOpType.mult),
                    reads=xrs + sres(mem, "mv"), writes=sres(mem, "xnb"))
            else:
                P.add("vector", lambda e, xb=xb: e.tensor_scalar(
                    xcb[:], xb, mv[:, 0:1], None, op0=AluOpType.subtract),
                    reads=xrs + sres(mem, "mv"), writes=sres(mem, "xcb"))
                P.add("vector", lambda e: e.tensor_scalar(
                    xnb[:], xcb[:], mv[:, 3:4], None, op0=AluOpType.mult),
                    reads=sres(mem, "xcb") + sres(mem, "mv"), writes=sres(mem, "xnb"))
            bank = 6 + (t % 2)
            for c in range(8):
                MM0 = lambda e, c=c, bank=bank: e.transpose(
                    pb[bank][:, 128 * c:128 * c + 128], xnb[:, 128 * c:128 * c + 128], identb[:])
                P.add("tensor", MM0, reads=sres(mem, "xnb") + sres(mem, "identb"), writes=PS(bank))
            for c in range(8):
                P.add("vector", lambda e, c=c, t=t, bank=bank: e.tensor_copy(
                    XnT[:, 2048 * c + 128 * t:2048 * c + 128 * t + 128],
                    pb[bank][:, 128 * c:128 * c + 128]),
                    reads=PS(bank), writes=sres(mem, "XnT", 2048 * c + 128 * t, 2048 * c + 128 * t + 128))
            if L == 1:
                fb = 4 + (t % 2)
                for c in range(8):
                    P.add("tensor", lambda e, c=c, fb=fb: e.transpose(
                        pk[fb][:, 128 * (c % 4):128 * (c % 4) + 128].rearrange("p (a b) -> p a b", a=1)[:, 0, :]
                        if False else pk[fb][:, 128 * (c % 4):128 * (c % 4) + 128],
                        xcb[:, 128 * c:128 * c + 128], identf[:]),
                        reads=sres(mem, "xcb") + sres(mem, "identf"), writes=PS(fb))
                    if c % 4 == 3:
                        P.add("vector", lambda e, c=c, fb=fb: e.tensor_copy(
                            xcT[:, 512 * (c // 4):512 * (c // 4) + 512], pk[fb][:, 0:512]),
                            reads=PS(fb), writes=sres(mem, "xcT"))
                for c in range(8):
                    MM(xcT[:, 128 * c:128 * c + 128], centT[:, 8 * c:8 * c + 8],
                       pk[2][:, 0:8], c == 0, c == 7,
                       sres(mem, "xcT") + sres(mem, "centT"), PS(2))
                gA = gat.ap()[:, 8 * t:8 * t + 8]
                gF = gfin.ap()[:, 8 * t:8 * t + 8]
                gAr = sres(mem, "gat", 8 * t, 8 * t + 8)
                gFr = sres(mem, "gfin", 8 * t, 8 * t + 8)
                sm = smax.ap()
                smr = sres(mem, "smax")
                P.add("vector", lambda e: e.tensor_scalar(
                    mv[:, 2:3], mv[:, 3:4], -1.0, None, op0=AluOpType.mult),
                    reads=sres(mem, "mv"), writes=sres(mem, "mv"))
                P.add("scalar", lambda e, gA=gA: e.activation(gA, pk[2][:, 0:8], AF.Exp,
                                                              scale=mv[:, 2:3]),
                      reads=PS(2) + sres(mem, "mv"), writes=gAr)
                P.add("vector", lambda e, gA=gA: e.tensor_scalar(
                    gA, gA, 1.0, None, op0=AluOpType.add), reads=gAr, writes=gAr)
                P.add("vector", lambda e, gA=gA: e.reciprocal(gA, gA), reads=gAr, writes=gAr)
                gsr = sres(mem, "gscr")
                cir = sres(mem, "cidx")
                V = "vector"
                P.add(V, lambda e, gA=gA, sm=sm: e.reduce_max(sm[:, 0:1], gA, axis=AX.X),
                      reads=gAr, writes=smr)
                P.add(V, lambda e, gA=gA, gF=gF, sm=sm: e.tensor_scalar(
                    gF, gA, sm[:, 0:1], None, op0=AluOpType.is_equal), reads=gAr + smr, writes=gFr)
                P.add(V, lambda e, gF=gF: e.scalar_tensor_tensor(
                    gF, gF, -1000.0, cidx[:], op0=AluOpType.mult, op1=AluOpType.add),
                    reads=gFr + cir, writes=gFr)
                P.add(V, lambda e, gF=gF, sm=sm: e.tensor_reduce(
                    sm[:, 1:2], gF, axis=AX.X, op=AluOpType.min), reads=gFr, writes=smr)
                P.add(V, lambda e, gF=gF, sm=sm: e.tensor_scalar(
                    gF, gF, sm[:, 1:2], None, op0=AluOpType.is_equal), reads=gFr + smr, writes=gFr)
                P.add(V, lambda e, gA=gA, gF=gF: e.scalar_tensor_tensor(
                    gscr[:], gF, -1e30, gA, op0=AluOpType.mult, op1=AluOpType.add),
                    reads=gAr + gFr, writes=gsr)
                P.add(V, lambda e, sm=sm: e.reduce_max(sm[:, 2:3], gscr[:], axis=AX.X),
                      reads=gsr, writes=smr)
                P.add(V, lambda e, sm=sm: e.tensor_scalar(
                    gscr[:], gscr[:], sm[:, 2:3], None, op0=AluOpType.is_equal),
                    reads=gsr + smr, writes=gsr)
                P.add(V, lambda e: e.scalar_tensor_tensor(
                    gscr[:], gscr[:], -1000.0, cidx[:], op0=AluOpType.mult, op1=AluOpType.add),
                    reads=gsr + cir, writes=gsr)
                P.add(V, lambda e, sm=sm: e.tensor_reduce(
                    sm[:, 3:4], gscr[:], axis=AX.X, op=AluOpType.min), reads=gsr, writes=smr)
                P.add(V, lambda e, sm=sm: e.tensor_scalar(
                    gscr[:], gscr[:], sm[:, 3:4], None, op0=AluOpType.is_equal),
                    reads=gsr + smr, writes=gsr)
                P.add(V, lambda e, gF=gF: e.tensor_tensor(gF, gF, gscr[:], op=AluOpType.add),
                      reads=gFr + gsr, writes=gFr)
                P.add(V, lambda e, sm=sm: e.tensor_tensor(
                    sm[:, 4:5], sm[:, 0:1], sm[:, 2:3], op=AluOpType.add), reads=smr, writes=smr)
                P.add(V, lambda e, sm=sm: e.reciprocal(sm[:, 5:6], sm[:, 4:5]),
                      reads=smr, writes=smr)
                P.add(V, lambda e, gA=gA, gF=gF, sm=sm: e.scalar_tensor_tensor(
                    gF, gA, sm[:, 5:6], gF, op0=AluOpType.mult, op1=AluOpType.mult),
                    reads=gAr + smr + gFr, writes=gFr)

    def emit_ffn0():
        NKG1 = 6
        for e_ in range(NE):
            tok0 = 256 * e_
            for gi in range(NKG1):
                cw = 512 if gi < 5 else 256
                buf = gi % 2
                for nm, dsrc in (("w1b", din["L0_w1"]), ("w3b", din["L0_w3"])):
                    dma("gpsimd",
                        mem.tensors[nm][0].ap().rearrange("p (b k m) -> p b k m", b=2, k=8)[:, buf, :, 0:cw],
                        dsrc[e_][:, 512 * gi:512 * gi + cw].rearrange("(k p) m -> p k m", p=128),
                        sres(mem, nm, buf * 4096, buf * 4096 + 4096),
                        sres(mem, nm, buf * 4096, buf * 4096 + 4096), tag=f"ffn0w{e_}")
                for mc in range(4 if gi < 5 else 2):
                    mct = 4 * gi + mc
                    bnk = mct % 2
                    bnk2 = 2 + mct % 2
                    for k in range(8):
                        MM(w1b.ap()[:, 4096 * buf + 512 * k + 128 * mc: 4096 * buf + 512 * k + 128 * mc + 128],
                           XnT[:, 2048 * k + tok0:2048 * k + tok0 + 256],
                           pk[bnk][:, 0:256], k == 0, k == 7,
                           sres(mem, "w1b", buf * 4096, buf * 4096 + 4096) +
                           sres(mem, "XnT", 2048 * k + tok0, 2048 * k + tok0 + 256),
                           PS(bnk))
                    for k in range(8):
                        MM(w3b.ap()[:, 4096 * buf + 512 * k + 128 * mc: 4096 * buf + 512 * k + 128 * mc + 128],
                           XnT[:, 2048 * k + tok0:2048 * k + tok0 + 256],
                           pk[bnk2][:, 0:256], k == 0, k == 7,
                           sres(mem, "w3b", buf * 4096, buf * 4096 + 4096) +
                           sres(mem, "XnT", 2048 * k + tok0, 2048 * k + tok0 + 256),
                           PS(bnk2))
                    hsap = hs0.ap()[:, 256 * (mct % 2):256 * (mct % 2) + 256]
                    hsres = sres(mem, "hs0", 256 * (mct % 2), 256 * (mct % 2) + 256)
                    P.add("scalar", lambda e, b=bnk, hsap=hsap: e.activation(hsap, pk[b][:, 0:256], AF.Silu),
                          reads=PS(bnk), writes=hsres)
                    P.add("vector", lambda e, b=bnk2, mct=mct, hsap=hsap: e.tensor_tensor(
                        hT0[:, 256 * mct:256 * mct + 256], hsap, pk[b][:, 0:256], op=AluOpType.mult),
                        reads=hsres + PS(bnk2), writes=sres(mem, "hT0", 256 * mct, 256 * mct + 256))
            for gi in range(NKG1):
                cw = 512 if gi < 5 else 256
                buf = gi % 2
                dma("gpsimd",
                    w2b.ap().rearrange("p (b k m) -> p b k m", b=2, k=4)[:, buf, 0:cw // 128, :],
                    din["L0_w2"][e_][512 * gi:512 * gi + cw, :].rearrange("(k p) m -> p k m", p=128),
                    sres(mem, "w2b", buf * 4096, buf * 4096 + 4096),
                    sres(mem, "w2b", buf * 4096, buf * 4096 + 4096), tag=f"ffn0w2{e_}")
                for j in range(2):
                    tt = 2 * e_ + j
                    pbk = [4 + j, 6 + j]
                    for kc in range(cw // 128):
                        mct = 4 * gi + kc
                        for nh in range(2):
                            MM(hT0[:, 256 * mct + 128 * j:256 * mct + 128 * j + 128],
                               w2b.ap()[:, 4096 * buf + 1024 * kc + 512 * nh:4096 * buf + 1024 * kc + 512 * nh + 512],
                               pk[pbk[nh]][:, 0:512], gi == 0 and kc == 0, gi == NKG1 - 1 and kc == (cw // 128) - 1,
                               sres(mem, "hT0", 256 * mct, 256 * mct + 256) +
                               sres(mem, "w2b", buf * 4096, buf * 4096 + 4096),
                               PS(pbk[nh]))
                for j in range(2):
                    if gi == NKG1 - 1:
                        tt = 2 * e_ + j
                        for nh in range(2):
                            P.add("vector", lambda e, tt=tt, nh=nh, pb_=4 + j + 2 * nh: e.tensor_copy(
                                g[:, 1024 * tt + 512 * nh:1024 * tt + 512 * nh + 512], pk[pb_][:, 0:512]),
                                reads=PS(4 + j + 2 * nh),
                                writes=sres(mem, "g", 1024 * tt + 512 * nh, 1024 * tt + 512 * nh + 512))

    def emit_gates():
        for t in range(NT):
            for k in range(8):
                MM(XnT[:, 2048 * k + 128 * t:2048 * k + 128 * t + 128],
                   centT[:, 8 * k:8 * k + 8], pk[2][:, 0:8], k == 0, k == 7,
                   sres(mem, "XnT", 2048 * k + 128 * t, 2048 * k + 128 * t + 128) +
                   sres(mem, "centT"), PS(2))
            gA = gat.ap()[:, 8 * t:8 * t + 8]
            gF = gfin.ap()[:, 8 * t:8 * t + 8]
            gAr = sres(mem, "gat", 8 * t, 8 * t + 8)
            gFr = sres(mem, "gfin", 8 * t, 8 * t + 8)
            sm = smax.ap()
            smr = sres(mem, "smax")
            P.add("scalar", lambda e, gA=gA: e.activation(gA, pk[2][:, 0:8], AF.Sigmoid),
                  reads=PS(2), writes=gAr)
            P.add("vector", lambda e, gA=gA, sm=sm: e.reduce_max(sm[:, 0:1], gA, axis=AX.X),
                  reads=gAr, writes=smr)
            P.add("vector", lambda e, gA=gA, gF=gF, sm=sm: e.tensor_scalar(
                gF, gA, sm[:, 0:1], None, op0=AluOpType.is_equal), reads=gAr + smr, writes=gFr)
            P.add("vector", lambda e, gA=gA, gF=gF: e.scalar_tensor_tensor(
                gF, gF, -1e30, gA, op0=AluOpType.mult, op1=AluOpType.add), reads=gAr + gFr, writes=gFr)
            P.add("vector", lambda e, gF=gF, sm=sm: e.reduce_max(sm[:, 1:2], gF, axis=AX.X),
                  reads=gFr, writes=smr)
            P.add("vector", lambda e, gA=gA, gF=gF, sm=sm: e.tensor_scalar(
                gF, gA, sm[:, 1:2], None, op0=AluOpType.is_ge), reads=gAr + smr, writes=gFr)
            P.add("vector", lambda e, sm=sm: e.tensor_tensor(
                sm[:, 2:3], sm[:, 0:1], sm[:, 1:2], op=AluOpType.add), reads=smr, writes=smr)
            P.add("vector", lambda e, sm=sm: e.reciprocal(sm[:, 3:4], sm[:, 2:3]),
                  reads=smr, writes=smr)
            P.add("vector", lambda e, gA=gA, gF=gF, sm=sm: e.scalar_tensor_tensor(
                gF, gA, sm[:, 3:4], gF, op0=AluOpType.mult, op1=AluOpType.mult),
                reads=gAr + smr + gFr, writes=gFr)

    def emit_moe():
        for e_ in range(NE):
            for q in range(4):
                dma("gpsimd",
                    W2sb.ap().rearrange("p (k m) -> p k m", k=32)[:, 8 * q:8 * q + 8, :],
                    din["L1_W2"][e_][1024 * q:1024 * q + 1024, :].rearrange("(k p) m -> p k m", p=128),
                    sres(mem, "W2sb", 8192 * q, 8192 * q + 8192),
                    sres(mem, "W2sb", 8192 * q, 8192 * q + 8192), tag=f"moeW2.{e_}")
            b2b = e_ % 2
            dma("gpsimd", b2row[0:1, 1024 * b2b:1024 * b2b + 1024].rearrange("a b -> a b"),
                din["L1_b2"][e_:e_ + 1, :],
                sres(mem, "b2row", 1024 * b2b, 1024 * b2b + 1024),
                sres(mem, "b2row", 1024 * b2b, 1024 * b2b + 1024))
            for grp in range(4):
                tok0 = 512 * grp
                for mg in range(8):
                    buf = mg % 2
                    dma("gpsimd",
                        w1mb.ap().rearrange("p (b k m) -> p b k m", b=2, k=8)[:, buf, :, :],
                        din["L1_W1"][e_][:, 512 * mg:512 * mg + 512].rearrange("(k p) m -> p k m", p=128),
                        sres(mem, "w1mb", buf * 4096, buf * 4096 + 4096),
                        sres(mem, "w1mb", buf * 4096, buf * 4096 + 4096), tag=f"moeW1.{e_}.{grp}")
                    for mc in range(4):
                        mct = 4 * mg + mc
                        bnk = mct % 2
                        for k in range(8):
                            MM(w1mb.ap()[:, 4096 * buf + 512 * k + 128 * mc:4096 * buf + 512 * k + 128 * mc + 128],
                               XnT[:, 2048 * k + tok0:2048 * k + tok0 + 512],
                               pk[bnk][:, 0:512], k == 0, k == 7,
                               sres(mem, "w1mb", buf * 4096, buf * 4096 + 4096) +
                               sres(mem, "XnT", 2048 * k + tok0, 2048 * k + tok0 + 512),
                               PS(bnk))
                        P.add("scalar", lambda e, b=bnk, mct=mct, e_=e_: e.activation(
                            hTm[:, 512 * mct:512 * mct + 512], pk[b][:, 0:512], AF.Gelu,
                            bias=b1sb[:, 32 * e_ + mct:32 * e_ + mct + 1]),
                            reads=PS(bnk) + sres(mem, "b1sb"),
                            writes=sres(mem, "hTm", 512 * mct, 512 * mct + 512))
                for j in range(4):
                    for nh in range(2):
                        MM(onesrow[0:1, 0:128],
                           b2row[0:1, 1024 * b2b + 512 * nh:1024 * b2b + 512 * nh + 512],
                           pk[2 * j + nh][:, 0:512], True, False,
                           sres(mem, "onesrow") + sres(mem, "b2row", 1024 * b2b, 1024 * b2b + 1024),
                           PS(2 * j + nh))
                for kc in range(32):
                    for j in range(4):
                        for nh in range(2):
                            MM(hTm[:, 512 * kc + 128 * j:512 * kc + 128 * j + 128],
                               W2sb.ap()[:, 1024 * kc + 512 * nh:1024 * kc + 512 * nh + 512],
                               pk[2 * j + nh][:, 0:512], False, kc == 31,
                               sres(mem, "hTm", 512 * kc, 512 * kc + 512) +
                               sres(mem, "W2sb", 1024 * kc, 1024 * kc + 1024),
                               PS(2 * j + nh))
                for j in range(4):
                    tt = 4 * grp + j
                    for nh in range(2):
                        gap = g.ap()[:, 1024 * tt + 512 * nh:1024 * tt + 512 * nh + 512]
                        gr = sres(mem, "g", 1024 * tt + 512 * nh, 1024 * tt + 512 * nh + 512)
                        gsc = gfin.ap()[:, 8 * tt + e_:8 * tt + e_ + 1]
                        gscr = sres(mem, "gfin", 8 * tt, 8 * tt + 8)
                        if e_ == 0:
                            P.add("vector", lambda e, pbk=2 * j + nh, gap=gap, gsc=gsc: e.tensor_scalar(
                                gap, pk[pbk][:, 0:512], gsc, None, op0=AluOpType.mult),
                                reads=PS(2 * j + nh) + gscr, writes=gr)
                        else:
                            P.add("vector", lambda e, pbk=2 * j + nh, gap=gap, gsc=gsc: e.scalar_tensor_tensor(
                                gap, pk[pbk][:, 0:512], gsc, gap, op0=AluOpType.mult, op1=AluOpType.add),
                                reads=PS(2 * j + nh) + gscr + gr, writes=gr)

    def emit_attn(L):
        sc_ = 0.125 if L == 0 else 1.0
        QTz = [QTz0, QTz1]
        KTA = [KTA0, KTA1]
        MA = [MA0, MA1]
        r_ = lambda nm, a=None, b=None: mem.res(nm, a, b)

        def attn_init():
            for hh in (0, 1):
                ob = 64 * (1 - hh)
                tts = [(f"QTz{hh}", QTz[hh]), (f"KTA{hh}", KTA[hh])]
                if L == 1:
                    tts.append((f"MA{hh}", MA[hh]))
                for nm, tt in tts:
                    P.add("vector", lambda e, tt=tt, ob=ob: e.memset(
                        tt.ap()[ob:ob + 64, :], 0.0), (), r_(nm))

        def pair_weights(p):
            qoff = 128 * p
            koff = 128 * p if L == 0 else 1024 + 128 * p
            voff = 128 * p if L == 0 else 2048 + 128 * p
            srcs = [("wqb", (din["L0_Wq"] if L == 0 else din["L1_wqkv"]), qoff),
                    ("wkb", (din["L0_Wk"] if L == 0 else din["L1_wqkv"]), koff),
                    ("wvb", (din["L0_Wv"] if L == 0 else din["L1_wqkv"]), voff)]
            for nm, src, off in srcs:
                dma("gpsimd",
                    mem.tensors[nm][0].ap().rearrange("p (k m) -> p k m", k=8),
                    src[:, off:off + 128].rearrange("(k p) m -> p k m", p=128),
                    sres(mem, nm), sres(mem, nm), tag=f"wqkv{L}.{p}")

        def proj_qk(p):
            bq_col = (16 if L else 0) + p
            bk_col = (24 if L else 8) + p
            for which, wbn, bcol, dsts in (("q", "wqb", bq_col, QTz),
                                           ("k", "wkb", bk_col, KTA)):
                wb = mem.tensors[wbn][0]
                for n in range(4):
                    bnk = n % 2
                    for k in range(8):
                        MM(wb.ap()[:, 128 * k:128 * k + 128],
                           XnT[:, 2048 * k + 512 * n:2048 * k + 512 * n + 512],
                           pk[bnk][:, 0:512], k == 0, k == 7,
                           sres(mem, wbn) +
                           sres(mem, "XnT", 2048 * k + 512 * n, 2048 * k + 512 * n + 512),
                           PS(bnk))
                    for hh in (0, 1):
                        dn = (f"QTz{hh}" if which == "q" else f"KTA{hh}")
                        P.add("vector", lambda e, d=dsts[hh], hh=hh, n=n, bnk=bnk, bcol=bcol:
                              e.tensor_scalar(
                                  d.ap()[64 * hh:64 * hh + 64, 512 * n:512 * n + 512],
                                  pk[bnk][64 * hh:64 * hh + 64, 0:512],
                                  bqk[64 * hh:64 * hh + 64, bcol:bcol + 1], None,
                                  op0=AluOpType.add),
                              reads=PS(bnk) + sres(mem, "bqk"),
                              writes=r_(dn, 512 * n, 512 * n + 512))
            if L == 1:
                for hh in (0, 1):
                    P.add("vector", lambda e, hh=hh: e.tensor_scalar(
                        MA[hh].ap()[64 * hh:64 * hh + 64, :],
                        QTz[hh].ap()[64 * hh:64 * hh + 64, :],
                        -2.0, None, op0=AluOpType.mult),
                        reads=r_(f"QTz{hh}"), writes=r_(f"MA{hh}"))

        def proj_v(p):
            for t in range(NT):
                bnk = 2 + t % 2
                for k in range(8):
                    MM(XnT[:, 2048 * k + 128 * t:2048 * k + 128 * t + 128],
                       wvb.ap()[:, 128 * k:128 * k + 128],
                       pk[bnk][:, 0:128], k == 0, k == 7,
                       sres(mem, "wvb") + sres(mem, "XnT", 2048 * k + 128 * t, 2048 * k + 128 * t + 128),
                       PS(bnk))
                P.add("vector", lambda e, t=t, bnk=bnk: e.tensor_copy(
                    Vsb[:, 128 * t:128 * t + 128], pk[bnk][:, 0:128]),
                    reads=PS(bnk), writes=sres(mem, "Vsb", 128 * t, 128 * t + 128))

        def aug_build(p):
            for side, srcs, sel, seed, dsts in (
                    ("q", QTz, selq, seedq, MA), ("k", KTA, selk, seedk, KTA)):
                seln = "selq" if side == "q" else "selk"
                seedn = "seedq" if side == "q" else "seedk"
                for hh in (0, 1):
                    hb = 64 * hh
                    P.add("scalar", lambda e, s=srcs[hh], hb=hb: e.activation(
                        sqsb.ap()[hb:hb + 64, :], s.ap()[hb:hb + 64, :], AF.Square),
                        reads=r_(f"QTz{hh}" if side == "q" else f"KTA{hh}"),
                        writes=r_("sqsb"))
                for hh in (0, 1):
                    ob = 64 * (1 - hh)
                    dn = f"MA{hh}" if side == "q" else f"KTA{hh}"
                    for n in range(4):
                        MM(seed[0:1, :], onesrow[0:1, 0:512], pk[2][ob:ob + 2, 0:512],
                           True, False, sres(mem, seedn) + sres(mem, "onesrow"), PS(2))
                        MM(sel[:, 2 * hh:2 * hh + 2], sqsb.ap()[:, 512 * n:512 * n + 512],
                           pk[2][ob:ob + 2, 0:512], False, True,
                           sres(mem, seln) + r_("sqsb"), PS(2))
                        P.add("vector", lambda e, d=dsts[hh], ob=ob, n=n: e.tensor_copy(
                            d.ap()[ob:ob + 2, 512 * n:512 * n + 512], pk[2][ob:ob + 2, 0:512]),
                            reads=PS(2), writes=r_(dn))

        def chunks_stage(p, t):
            Lrow = 128 * (t + 1)
            par = t % 2
            nq = (Lrow + 511) // 512
            for cq in range(nq):
              for hh in (0, 1):
                sc_t = scor[(hh, par)]
                snm = f"scor{hh}" if par == 0 else f"scorX{hh}"
                c0 = 512 * cq
                cw = min(512, Lrow - c0)
                if L == 0:
                    bnk = 2 * hh + cq % 2
                    MM(QTz[hh].ap()[:, 128 * t:128 * t + 128],
                       KTA[hh].ap()[:, c0:c0 + cw],
                       pk[bnk][:, 0:cw], True, True,
                       r_(f"QTz{hh}", 128 * t, 128 * t + 128) + r_(f"KTA{hh}", c0, c0 + cw),
                       PS(bnk))
                    if c0 + cw == Lrow:
                        P.add("vector", lambda e, bnk=bnk, o=cw - 128: e.tensor_tensor(
                            pk[bnk][:, o:o + 128], pk[bnk][:, o:o + 128], mask0[:], op=AluOpType.add),
                            reads=PS(bnk) + sres(mem, "mask0"), writes=PS(bnk))
                    P.add("scalar", lambda e, bnk=bnk, sc_t=sc_t, c0=c0, cw=cw: e.copy(
                        sc_t.ap()[:, c0:c0 + cw], pk[bnk][:, 0:cw]),
                        reads=PS(bnk), writes=sres(mem, snm, c0, c0 + cw))
                else:
                    bd = 2 * hh
                    bn = 2 * hh + 1
                    MM(QTz[hh].ap()[:, 128 * t:128 * t + 128], KTA[hh].ap()[:, c0:c0 + cw],
                       pk[bd][:, 0:cw], True, True,
                       r_(f"QTz{hh}", 128 * t, 128 * t + 128) + r_(f"KTA{hh}", c0, c0 + cw),
                       PS(bd))
                    MM(MA[hh].ap()[:, 128 * t:128 * t + 128], KTA[hh].ap()[:, c0:c0 + cw],
                       pk[bn][:, 0:cw], True, True,
                       r_(f"MA{hh}", 128 * t, 128 * t + 128) + r_(f"KTA{hh}", c0, c0 + cw),
                       PS(bn))
                    P.add("scalar", lambda e, bd=bd, sc_t=sc_t, c0=c0, cw=cw: e.activation(
                        sc_t.ap()[:, c0:c0 + cw], pk[bd][:, 0:cw], AF.Square),
                        reads=PS(bd), writes=sres(mem, snm, c0, c0 + cw))
                    P.add("scalar", lambda e, bn=bn, hh=hh, cw=cw: e.activation(
                        rdb[hh].ap()[:, 0:cw], pk[bn][:, 0:cw], AF.Ln),
                        reads=PS(bn), writes=sres(mem, f"rd{hh}"))
                    P.add("scalar", lambda e, hh=hh, cw=cw: e.activation(
                        rdb[hh].ap()[:, 0:cw], rdb[hh].ap()[:, 0:cw], AF.Exp, scale=-1.0),
                        reads=sres(mem, f"rd{hh}"), writes=sres(mem, f"rd{hh}"))
                    P.add("vector", lambda e, hh=hh, sc_t=sc_t, c0=c0, cw=cw: e.tensor_tensor(
                        sc_t.ap()[:, c0:c0 + cw], sc_t.ap()[:, c0:c0 + cw],
                        rdb[hh].ap()[:, 0:cw], op=AluOpType.mult),
                        reads=sres(mem, snm, c0, c0 + cw) + sres(mem, f"rd{hh}"),
                        writes=sres(mem, snm, c0, c0 + cw))
                    if c0 + cw == Lrow:
                        P.add("vector", lambda e, sc_t=sc_t, o=Lrow - 128: e.tensor_tensor(
                            sc_t.ap()[:, o:o + 128], sc_t.ap()[:, o:o + 128],
                            mask1[:], op=AluOpType.add),
                            reads=sres(mem, snm, Lrow - 128, Lrow) + sres(mem, "mask1"),
                            writes=sres(mem, snm, Lrow - 128, Lrow))

        def tails_stage(p, t):
            Lrow = 128 * (t + 1)
            par = t % 2
            for hh in (0, 1):
                sc_t = scor[(hh, par)]
                snm = f"scor{hh}" if par == 0 else f"scorX{hh}"
                Eb = Ebuf[(hh, par)]
                Enm = f"E{hh}{par}"
                sb_ = 8 * (2 * par + hh)
                P.add("vector", lambda e, sc_t=sc_t, Lr=Lrow, sb_=sb_: e.reduce_max(
                    smax[:, sb_:sb_ + 1], sc_t.ap()[:, 0:Lr], axis=AX.X, negate=True),
                    reads=sres(mem, snm, 0, Lrow), writes=sres(mem, "smax"))
                P.add("vector", lambda e, sb_=sb_: e.tensor_scalar(
                    smax[:, sb_ + 1:sb_ + 2], smax[:, sb_:sb_ + 1], sc_, None, op0=AluOpType.mult),
                    reads=sres(mem, "smax"), writes=sres(mem, "smax"))
                P.add("scalar", lambda e, sc_t=sc_t, Eb=Eb, Lr=Lrow, sb_=sb_, Enm=Enm: e.activation(
                    Eb.ap()[:, 0:Lr], sc_t.ap()[:, 0:Lr], AF.Exp,
                    bias=smax[:, sb_ + 1:sb_ + 2], scale=sc_,
                    accum_out=smax[:, sb_ + 2:sb_ + 3]),
                    reads=sres(mem, snm, 0, Lrow) + sres(mem, "smax"),
                    writes=sres(mem, Enm, 0, Lrow) + sres(mem, "smax"))
                P.add("vector", lambda e, sb_=sb_: e.reciprocal(
                    smax[:, sb_ + 3:sb_ + 4], smax[:, sb_ + 2:sb_ + 3]),
                    reads=sres(mem, "smax"), writes=sres(mem, "smax"))
                P.add("vector", lambda e, Eb=Eb, Enm=Enm, Lr=Lrow, sb_=sb_: e.tensor_scalar(
                    Eb.ap()[:, 0:Lr], Eb.ap()[:, 0:Lr], smax[:, sb_ + 3:sb_ + 4],
                    None, op0=AluOpType.mult),
                    reads=sres(mem, Enm, 0, Lrow) + sres(mem, "smax"),
                    writes=sres(mem, Enm, 0, Lrow))

        def txav_stage(p, t, hh):
            par = t % 2
            Eb = Ebuf[(hh, par)]
            Enm = f"E{hh}{par}"
            ntile = t + 1
            for grp in range((ntile + 7) // 8):
                bnk = 6 + grp % 2
                ntg = min(8, ntile - 8 * grp)
                for j in range(ntg):
                    kt = 8 * grp + j
                    P.add("tensor", lambda e, Eb=Eb, kt=kt, j=j, bnk=bnk: e.transpose(
                        pb[bnk][:, 128 * j:128 * j + 128],
                        Eb.ap()[:, 128 * kt:128 * kt + 128], identb[:]),
                        reads=sres(mem, Enm, 128 * kt, 128 * kt + 128) + sres(mem, "identb"),
                        writes=PS(bnk))
                P.add("vector", lambda e, hh=hh, grp=grp, bnk=bnk, ntg=ntg: e.tensor_copy(
                    ETb[hh].ap()[:, 1024 * grp:1024 * grp + 128 * ntg], pb[bnk][:, 0:128 * ntg]),
                    reads=PS(bnk),
                    writes=sres(mem, f"ET{hh}", 1024 * grp, 1024 * grp + 128 * ntg))
            avb = 4 + hh
            for kt in range(ntile):
                MM(Vsb[:, 128 * kt + 64 * hh:128 * kt + 64 * hh + 64],
                   ETb[hh].ap()[:, 128 * kt:128 * kt + 128],
                   pk[avb][0:64, 0:128], kt == 0, kt == ntile - 1,
                   sres(mem, "Vsb", 128 * kt, 128 * kt + 128) +
                   sres(mem, f"ET{hh}", 128 * kt, 128 * kt + 128),
                   PS(avb))
            P.add("vector", lambda e, hh=hh, p=p, t=t, avb=avb: e.tensor_copy(
                OTall[64 * hh:64 * hh + 64, 2048 * p + 128 * t:2048 * p + 128 * t + 128],
                pk[avb][0:64, 0:128]),
                reads=PS(avb),
                writes=sres(mem, "OTall", 2048 * p + 128 * t, 2048 * p + 128 * t + 128))

        attn_init()
        pair_weights(0)
        proj_qk(0)
        proj_v(0)
        if L == 1:
            aug_build(0)
        for p in range(NPAIR):
            for t in range(NT):
                if t >= 2:
                    for hh in (0, 1):
                        txav_stage(p, t - 2, hh)
                chunks_stage(p, t)
                if t >= 1:
                    tails_stage(p, t - 1)
                if t == 2 and p + 1 < NPAIR:
                    pair_weights(p + 1)
            tails_stage(p, NT - 1)
            if p + 1 < NPAIR:
                proj_qk(p + 1)
            for t in (NT - 2, NT - 1):
                for hh in (0, 1):
                    txav_stage(p, t, hh)
            if p + 1 < NPAIR:
                proj_v(p + 1)
                if L == 1:
                    aug_build(p + 1)

        wo_src = din["L0_Wo"] if L == 0 else din["L1_wout"]
        for q in range(2):
            dma("gpsimd",
                Wosb.ap().rearrange("p (k m) -> p k m", k=8)[:, 4 * q:4 * q + 4, :],
                wo_src[512 * q:512 * q + 512, :].rearrange("(k p) m -> p k m", p=128),
                sres(mem, "Wosb", 4096 * q, 4096 * q + 4096),
                sres(mem, "Wosb", 4096 * q, 4096 * q + 4096), tag=f"wo{L}")
        bv_src = din["L0_bv"] if L == 0 else din["L1_bqkv"][2048:3072]
        bo_src = din["L0_bo"] if L == 0 else din["L1_bout"]
        dma("sync", btmp[0:8, :], bv_src.rearrange("(a b) -> a b", b=128),
            sres(mem, "btmp"), sres(mem, "btmp"))
        P.add("tensor", lambda e: e.transpose(pk[7][0:128, 0:8], btmp[0:8, :], identf[0:8, 0:8]),
              reads=sres(mem, "btmp") + sres(mem, "identf"), writes=PS(7))
        P.add("vector", lambda e: e.tensor_copy(bvc[:], pk[7][:, 0:8]),
              reads=PS(7), writes=sres(mem, "bvc"))
        dma("gpsimd", borow[0:1, :], bo_src.rearrange("(a b) -> a b", a=1),
            sres(mem, "borow"), sres(mem, "borow"))
        for nh in range(2):
            MM(onesrow[0:1, 0:1], borow[0:1, 512 * nh:512 * nh + 512], pk[6][0:1, 0:512] if nh == 0 else pk[7][0:1, 0:512],
               True, False, sres(mem, "onesrow") + sres(mem, "borow"), PS(6 + nh))
            for k in range(8):
                MM(bvc[:, k:k + 1], Wosb.ap()[:, 1024 * k + 512 * nh:1024 * k + 512 * nh + 512],
                   pk[6 + nh][0:1, 0:512], False, k == 7,
                   sres(mem, "bvc") + sres(mem, "Wosb"), PS(6 + nh))
            P.add("vector", lambda e, nh=nh: e.tensor_copy(
                bprow[0:1, 512 * nh:512 * nh + 512], (pk[6 + nh])[0:1, 0:512]),
                reads=PS(6 + nh), writes=sres(mem, "bprow"))
        for nh in range(2):
            MM(onesrow[0:1, 0:128], bprow[0:1, 512 * nh:512 * nh + 512], pk[6 + nh][:, 0:512],
               True, True, sres(mem, "onesrow") + sres(mem, "bprow"), PS(6 + nh))
            P.add("vector", lambda e, nh=nh: e.tensor_copy(
                bobc[:, 512 * nh:512 * nh + 512], pk[6 + nh][:, 0:512]),
                reads=PS(6 + nh), writes=sres(mem, "bobc", 512 * nh, 512 * nh + 512))

        xsrc = X0 if L == 0 else dx1
        xdst = dx1 if L == 0 else out
        xres_r = (lambda t: (f"dx0c.{t}",)) if L == 0 else (lambda t: (f"dx1.{t}",))
        xres_w = (lambda t: (f"dx1.{t}",)) if L == 0 else (lambda t: (f"dout.{t}",))
        for t in range(NT):
            xb = xbuf.ap()[:, (t % 2) * D:(t % 2) * D + D]
            xrs = sres(mem, "xbuf", (t % 2) * D, (t % 2) * D + D)
            dma("sync", xb, xsrc[128 * t:128 * t + 128, :], xres_r(t), xrs, tag=f"cmbx{L}.{t}")
            for nh in range(2):
                bnk = nh
                for k in range(8):
                    MM(OTall[:, 2048 * k + 128 * t:2048 * k + 128 * t + 128],
                       Wosb.ap()[:, 1024 * k + 512 * nh:1024 * k + 512 * nh + 512],
                       pk[bnk][:, 0:512], k == 0, k == 7,
                       sres(mem, "OTall", 2048 * k + 128 * t, 2048 * k + 128 * t + 128) +
                       sres(mem, "Wosb"),
                       PS(bnk))
            P.add("scalar", lambda e, t=t: e.activation(gs[:], g[:, 1024 * t:1024 * t + 1024], AF.Silu),
                  reads=sres(mem, "g", 1024 * t, 1024 * t + 1024), writes=sres(mem, "gs"))
            for nh in range(2):
                P.add("vector", lambda e, nh=nh: e.tensor_tensor(
                    t1[:, 512 * nh:512 * nh + 512], pk[nh][:, 0:512],
                    bobc[:, 512 * nh:512 * nh + 512], op=AluOpType.add),
                    reads=PS(nh) + sres(mem, "bobc", 512 * nh, 512 * nh + 512),
                    writes=sres(mem, "t1", 512 * nh, 512 * nh + 512))
            P.add("vector", lambda e: e.tensor_tensor(t1[:], t1[:], gs[:], op=AluOpType.mult),
                  reads=sres(mem, "t1") + sres(mem, "gs"), writes=sres(mem, "t1"))
            P.add("vector", lambda e, xb=xb: e.tensor_tensor(xb, t1[:], xb, op=AluOpType.add),
                  reads=sres(mem, "t1") + xrs, writes=xrs)
            dma("sync", xdst[128 * t:128 * t + 128, :], xb, xrs, xres_w(t), tag=f"cmbo{L}.{t}")

    if 0 not in layers:
        for t in range(NT):
            xb = xbuf.ap()[:, (t % 2) * D:(t % 2) * D + D]
            xrs = sres(mem, "xbuf", (t % 2) * D, (t % 2) * D + D)
            dma("sync", xb, X0[128 * t:128 * t + 128, :], (f"dx0.{t}",), xrs)
            dma("sync", dx1[128 * t:128 * t + 128, :], xb, xrs, (f"dx1.{t}",))
    if 0 in layers:
        emit_ln(0)
        emit_ffn0()
        emit_attn(0)
    if 1 in layers:
        emit_ln(1)
        emit_moe()
        emit_attn(1)
    if 1 not in layers:
        for t in range(NT):
            xb = xbuf.ap()[:, (t % 2) * D:(t % 2) * D + D]
            xrs = sres(mem, "xbuf", (t % 2) * D, (t % 2) * D + D)
            dma("sync", xb, dx1[128 * t:128 * t + 128, :], (f"dx1.{t}",), xrs)
            dma("sync", out[128 * t:128 * t + 128, :], xb, xrs, (f"dout.{t}",))

    for tname, bufname, cols, dt in taps:
        tsr = sres(mem, bufname, 0, cols)
        tb, _, _ = mem.tensors[bufname]
        npart = tb.shape[0]
        tout = nc.dram_tensor("tap_" + tname, [npart, cols], dt, kind="ExternalOutput").ap()
        dma("sync", tout, tb.ap()[:, 0:cols], tsr, (f"tap.{tname}",))

    P.finalize(nc, block, sems)
    stack.close()
    return nc


def make_consts():
    tri0 = np.triu(np.ones((128, 128), np.float32), k=1)
    selq = np.zeros((128, 4), np.float32)
    selq[0:64, 1] = 1.0
    selq[64:128, 3] = 1.0
    selk = np.zeros((128, 4), np.float32)
    selk[0:64, 0] = 1.0
    selk[64:128, 2] = 1.0
    return {
        "c_ident_bf": np.eye(128).astype(BF16NP),
        "c_ident_f32": np.eye(128).astype(np.float32),
        "c_mask0": (tri0 * -1e30).astype(np.float32),
        "c_mask1": (tri0 * -1e9).astype(np.float32),
        "c_ones_row": np.ones((1, 512), BF16NP),
        "c_seedq": np.array([[1.0, 1e-6]], BF16NP),
        "c_seedk": np.array([[0.0, 1.0]], BF16NP),
        "c_selq": selq.astype(BF16NP),
        "c_selk": selk.astype(BF16NP),
        "c_idx": (1000.0 + np.arange(8, dtype=np.float32))[None, :].repeat(128, 0),
    }


def make_in_maps(inputs, layers=(0, 1)):
    consts = make_consts()
    base = dict(consts)
    f32c = lambda a: np.ascontiguousarray(np.asarray(a), dtype=np.float32)
    if 0 in layers:
        base.update({
            "L0_Wq": f32c(inputs["L0_attn_Wq"]), "L0_Wk": f32c(inputs["L0_attn_Wk"]),
            "L0_Wv": f32c(inputs["L0_attn_Wv"]), "L0_Wo": f32c(inputs["L0_attn_Wo"]),
            "L0_bq": f32c(inputs["L0_attn_bq"]), "L0_bk": f32c(inputs["L0_attn_bk"]),
            "L0_bv": f32c(inputs["L0_attn_bv"]), "L0_bo": f32c(inputs["L0_attn_bo"]),
            "L0_w1": f32c(inputs["L0_ffn_w1"]), "L0_w3": f32c(inputs["L0_ffn_w3"]),
            "L0_w2": f32c(inputs["L0_ffn_w2"]),
        })
    if 1 in layers:
        base.update({
            "L1_wqkv": f32c(inputs["L1_attn_w_qkv"]), "L1_bqkv": f32c(inputs["L1_attn_b_qkv"]),
            "L1_wout": f32c(inputs["L1_attn_w_out"]), "L1_bout": f32c(inputs["L1_attn_b_out"]),
            "L1_cent": f32c(inputs["L1_ffn_centroids"]), "L1_W1": f32c(inputs["L1_ffn_W1"]),
            "L1_b1": f32c(inputs["L1_ffn_b1"]), "L1_W2": f32c(inputs["L1_ffn_W2"]),
            "L1_b2": f32c(inputs["L1_ffn_b2"]),
        })
    X = f32c(inputs["L0_attn_X"])
    maps = []
    for c in range(8):
        m = dict(base)
        m["x0"] = np.ascontiguousarray(X[c])
        maps.append(m)
    return maps


def kernel(**inputs):
    nc = build_nc(layers=(0, 1))
    maps = make_in_maps(inputs, layers=(0, 1))
    res = run_bass_kernel_spmd(nc, maps, core_ids=list(range(8)))
    return np.stack([np.asarray(r["out"]) for r in res.results], axis=0)
